# version 17
# baseline (speedup 1.0000x reference)
import contextlib
import numpy as np
import ml_dtypes
import concourse.bass as bass
import concourse.mybir as mybir
from concourse.ap import AP
from concourse.bass_utils import run_bass_kernel_spmd

F32 = mybir.dt.float32
BF16 = mybir.dt.bfloat16
AOT = mybir.AluOpType
ACTF = mybir.ActivationFunctionType
AX = mybir.AxisListType

N = 2048
E = 2048
H = 16
D = 128
MC = 16
KC = 16
NB = 4
EPS = 1e-5


class Op:
    __slots__ = ("eng", "fn", "deps", "inc", "slot", "use", "idx", "cum",
                 "dma_waits", "drain_upto")

    def __init__(self, eng, fn, idx):
        self.eng = eng
        self.fn = fn
        self.deps = []
        self.inc = False
        self.slot = None
        self.use = 0
        self.idx = idx
        self.cum = 0
        self.dma_waits = ()
        self.drain_upto = -1


class Prog:
    ENGS = ("t", "v", "s", "g", "y")

    def __init__(self):
        self.streams = {e: [] for e in self.ENGS}
        self.writer = {}
        self.readers = {}
        self.slots = {}

    def _dep(self, op, dep):
        if dep is op or dep is None:
            return
        if dep.slot is None and dep.eng == op.eng and (
                dep.eng in ("t", "g", "y") or dep.fn is None):
            return
        if dep.slot is not None:
            val = 16 * self.slots[dep.slot]
            for i, (d, v) in enumerate(op.deps):
                if d is dep:
                    if val > v:
                        op.deps[i] = (dep, val)
                    return
            op.deps.append((dep, val))
            return
        for d, _ in op.deps:
            if d is dep:
                return
        op.deps.append((dep, None))

    def _record(self, eng, fn, reads, writes, slot, dma_waits):
        op = Op(eng, fn, len(self.streams[eng]))
        if slot is not None:
            op.slot = slot
        op.dma_waits = tuple(dma_waits)
        for k in reads:
            w = self.writer.get(k)
            if w is not None:
                self._dep(op, w)
        for k in writes:
            rd = self.readers.get(k)
            if rd:
                for r in rd.values():
                    self._dep(op, r)
            w = self.writer.get(k)
            if w is not None:
                self._dep(op, w)
        if slot is not None:
            self.slots[slot] = self.slots.get(slot, 0) + 1
            op.use = self.slots[slot]
        for k in reads:
            self.readers.setdefault(k, {})[eng] = op
        for k in writes:
            self.writer[k] = op
            self.readers[k] = {}
        self.streams[eng].append(op)
        return op

    def op(self, eng, fn, reads=(), writes=()):
        assert eng in ("t", "v", "s")
        return self._record(eng, fn, reads, writes, None, ())

    def dma(self, ring, fn, reads=(), writes=(), slot=None):
        assert ring in ("g", "y")
        if slot is None:
            slot = "dma:" + (writes[0] if writes else "anon")
        return self._record(ring, fn, reads, writes, ring + ":" + slot, ())

    def barrier(self, eng, slot_waits, writes=()):
        waits = [(k, 16 * self.slots.get(k, 0)) for k in slot_waits]
        return self._record(eng, None, (), writes, None, waits)

    def fence(self):
        lasts = {e: (self.streams[e][-1] if self.streams[e] else None)
                 for e in self.ENGS}
        slot_totals = [(k, 16 * v) for k, v in self.slots.items()]
        for e in self.ENGS:
            op = self._record(e, None, (), (), None, slot_totals)
            for e2, lo in lasts.items():
                if lo is not None and e2 != e and lo.slot is None:
                    self._dep(op, lo)

    def finalize_marks(self):
        for e in self.ENGS:
            for op in self.streams[e]:
                for dep, _ in op.deps:
                    dep.inc = True
        for e in ("t", "v", "s"):
            c = 0
            for op in self.streams[e]:
                if op.inc:
                    c += 1
                op.cum = c

    def emit_one(self, e, eng, sems, slot_sems):
        waited = {}
        drained = -1
        for i, op in enumerate(self.streams[e]):
            for sk, val in op.dma_waits:
                hs = slot_sems[sk]
                if waited.get(id(hs), -1) < val:
                    eng.wait_ge(hs, val)
                    waited[id(hs)] = val
            for dep, dval in op.deps:
                if dep.slot is not None:
                    hs = slot_sems[dep.slot]
                    val = dval
                else:
                    hs = sems[dep.eng]
                    val = dep.cum
                if waited.get(id(hs), -1) < val:
                    eng.wait_ge(hs, val)
                    waited[id(hs)] = val
            if op.drain_upto > drained:
                eng.drain()
                drained = i - 1
            if op.fn is None:
                if op.inc and e in sems:
                    eng.nop().then_inc(sems[e], 1)
                continue
            inst = op.fn(eng)
            if op.slot is not None:
                inst.then_inc(slot_sems[op.slot], 16)
            elif op.inc:
                inst.then_inc(sems[e], 1)


def host_consts():
    inv = 1.0 / (10000.0 ** (np.arange(0, D, 2, dtype=np.float32) / D))
    t = np.arange(N, dtype=np.float32)[:, None] * inv[None, :]
    cos = np.cos(t).astype(np.float32)
    sin = np.sin(t).astype(np.float32)
    return {
        "ctab": np.ascontiguousarray(
            np.concatenate([cos, cos], 1).astype(ml_dtypes.bfloat16)),
        "stab": np.ascontiguousarray(
            np.concatenate([-sin, sin], 1).astype(ml_dtypes.bfloat16)),
        "tri": np.triu(np.ones((128, 128), np.float32)).astype(ml_dtypes.bfloat16),
    }


def build_nc():
    nc = bass.Bass("TRN2", target_bir_lowering=False)

    xh = nc.dram_tensor("xb", [N, E], F32, kind="ExternalInput")
    wqkv = nc.dram_tensor("wqkv", [E, 3 * E], F32, kind="ExternalInput")
    wout = nc.dram_tensor("wout", [E, E], F32, kind="ExternalInput")
    gam = nc.dram_tensor("gam", [D], F32, kind="ExternalInput")
    bet = nc.dram_tensor("bet", [D], F32, kind="ExternalInput")
    fw = nc.dram_tensor("fw", [E, E], F32, kind="ExternalInput")
    fb = nc.dram_tensor("fb", [E], F32, kind="ExternalInput")
    ctabh = nc.dram_tensor("ctab", [N, D], BF16, kind="ExternalInput")
    stabh = nc.dram_tensor("stab", [N, D], BF16, kind="ExternalInput")
    trih = nc.dram_tensor("tri", [128, 128], BF16, kind="ExternalInput")
    out_h = nc.dram_tensor("out", [N, E], F32, kind="ExternalOutput")
    attnt_h = nc.dram_tensor("attnt_sc", [E, N], BF16, kind="Internal")
    out1_h = nc.dram_tensor("out1_sc", [N, E], F32, kind="Internal")
    lnx2t_h = nc.dram_tensor("lnx2t_sc", [E, N], BF16, kind="Internal")
    swt_h = nc.dram_tensor("swt_sc", [E, N], BF16, kind="Internal")

    P = Prog()
    with contextlib.ExitStack() as top:
        TE = top.enter_context
        LNXT = TE(nc.sbuf_tensor("LNXT", [128, KC * N], BF16))
        GAM = TE(nc.sbuf_tensor("GAM", [128, 1], F32))
        BET = TE(nc.sbuf_tensor("BET", [128, 1], F32))
        BIASF = TE(nc.sbuf_tensor("BIASF", [128, 16], F32))
        CP = TE(nc.sbuf_tensor("CP", [128, 8], F32))
        ALPHA = TE(nc.sbuf_tensor("ALPHA", [128, 16], F32))
        ALPHACC = TE(nc.sbuf_tensor("ALPHACC", [128, 16], F32))

        C_ZERO, C_I128, C_I2048, C_EPS, C_TWO, C_ONE, C_NI2048 = \
            0, 1, 2, 3, 4, 5, 6

        def cp(col):
            return CP[:, col:col + 1]

        def ld(dst_ap, src_ap, key, slow=False):
            P.dma("g", lambda g, d=dst_ap, s=src_ap, sl=slow: g.dma_start(
                out=d, in_=s, allow_slow_non_contiguous=sl),
                writes=[key], slot=key)

        ld(GAM[:], AP(gam, 0, [[1, 128], [1, 1]]), "GAM", slow=True)
        ld(BET[:], AP(bet, 0, [[1, 128], [1, 1]]), "BET", slow=True)
        ld(BIASF[:], AP(fb, 0, [[1, 128], [128, 16]]), "BIASF", slow=True)
        for col, val in [(C_ZERO, 0.0), (C_I128, 1.0 / 128.0),
                         (C_I2048, 1.0 / 2048.0), (C_EPS, EPS),
                         (C_TWO, 2.0), (C_ONE, 1.0),
                         (C_NI2048, -1.0 / 2048.0)]:
            P.op("v", lambda v, c=col, vv=val: v.memset(CP[:, c:c + 1], vv),
                 writes=[f"CP{col}"])

        def ln_smalls(S, kp, sumk, ssk, b0=0):
            P.op("v", lambda v: v.tensor_scalar(
                S[:, b0 + 2:b0 + 3], S[:, b0:b0 + 1], cp(C_I2048), None,
                AOT.mult, AOT.bypass), reads=[sumk, "CP2"], writes=[kp + "m"])
            P.op("v", lambda v: v.tensor_scalar(
                S[:, b0 + 3:b0 + 4], S[:, b0 + 1:b0 + 2], cp(C_I2048), None,
                AOT.mult, AOT.bypass), reads=[ssk], writes=[kp + "e"])
            P.op("v", lambda v: v.tensor_tensor(
                S[:, b0 + 4:b0 + 5], S[:, b0 + 2:b0 + 3], S[:, b0 + 2:b0 + 3],
                AOT.mult), reads=[kp + "m"], writes=[kp + "q"])
            P.op("v", lambda v: v.tensor_tensor(
                S[:, b0 + 4:b0 + 5], S[:, b0 + 3:b0 + 4], S[:, b0 + 4:b0 + 5],
                AOT.subtract), reads=[kp + "e", kp + "q"], writes=[kp + "q"])
            P.op("s", lambda s: s.activation(
                S[:, b0 + 5:b0 + 6], S[:, b0 + 4:b0 + 5], ACTF.Sqrt,
                bias=cp(C_EPS)), reads=[kp + "q", "CP3"], writes=[kp + "sd"])
            P.op("v", lambda v: v.reciprocal(
                S[:, b0 + 6:b0 + 7], S[:, b0 + 5:b0 + 6]),
                 reads=[kp + "sd"], writes=[kp + "r"])

        with contextlib.ExitStack() as sA:
            TA = sA.enter_context
            CT = TA(nc.sbuf_tensor("CT", [128, MC * 128], BF16))
            ST = TA(nc.sbuf_tensor("ST", [128, MC * 128], BF16))
            TRI = TA(nc.sbuf_tensor("TRI", [128, 128], BF16))
            DEN = [TA(nc.sbuf_tensor(f"DEN{i}", [128, 16], F32)) for i in range(2)]
            RINV = [TA(nc.sbuf_tensor(f"RINV{i}", [128, 16], F32)) for i in range(2)]
            OUTH = [TA(nc.sbuf_tensor(f"OUTH{i}", [128, N], BF16)) for i in range(2)]
            AFM = [TA(nc.sbuf_tensor(f"AFM{i}", [128, N], BF16)) for i in range(2)]
            ld(CT[:], AP(ctabh, 0, [[128, 128], [128 * 128, 16], [1, 128]]), "CT")
            ld(ST[:], AP(stabh, 0, [[128, 128], [128 * 128, 16], [1, 128]]), "ST")
            ld(TRI[:], trih.ap(), "TRI")
            XB = [TA(nc.sbuf_tensor(f"XB{i}", [128, E], F32)) for i in range(2)]
            LNC = [TA(nc.sbuf_tensor(f"LNC{i}", [128, E], BF16)) for i in range(2)]
            XS = TA(nc.sbuf_tensor("XS", [128, 8], F32))
            SCR = TA(nc.sbuf_tensor("SCR", [128, E], BF16))
            WH = [TA(nc.sbuf_tensor(f"WH{i}", [128, KC * 384], BF16)) for i in range(2)]
            QR = TA(nc.sbuf_tensor("QR", [128, MC * 256], F32))
            T1 = [TA(nc.sbuf_tensor(f"T1{i}", [128, 256], F32)) for i in range(2)]
            T2 = [TA(nc.sbuf_tensor(f"T2{i}", [128, 256], F32)) for i in range(2)]
            SUMS = TA(nc.sbuf_tensor("SUMS", [128, 32], F32))
            SS = TA(nc.sbuf_tensor("SS", [128, 32], F32))
            MEANS = TA(nc.sbuf_tensor("MEANS", [128, 32], F32))
            RQ = TA(nc.sbuf_tensor("RQ", [128, 32], F32))
            SMA = TA(nc.sbuf_tensor("SMA", [128, 32], F32))
            SMB = TA(nc.sbuf_tensor("SMB", [128, 32], F32))
            QN = TA(nc.sbuf_tensor("QN", [128, MC * 256], BF16))
            P4S = [TA(nc.sbuf_tensor(f"P4S{i}", [128, 128], F32)) for i in range(4)]
            QFM = [TA(nc.sbuf_tensor(f"QFM{i}", [128, N], BF16)) for i in range(2)]
            KFM = [TA(nc.sbuf_tensor(f"KFM{i}", [128, N], BF16)) for i in range(2)]
            VPAD = [TA(nc.sbuf_tensor(f"VPAD{i}", [128, MC * 129], BF16))
                    for i in range(2)]
            S2B = [TA(nc.sbuf_tensor(f"S2B{i}", [128, 512], BF16)) for i in range(4)]
            S4B = [TA(nc.sbuf_tensor(f"S4B{i}", [128, 512], BF16)) for i in range(6)]
            PQ = [TA(nc.psum_tensor(f"PQ{i}", [128, 512], F32)) for i in range(2)]
            PS = [TA(nc.psum_tensor(f"PS{i}", [128, 512], F32)) for i in range(2)]
            PV = [TA(nc.psum_tensor(f"PV{i}", [128, 512], F32)) for i in range(4)]

            for i in range(2):
                P.op("v", lambda v, i=i: v.memset(
                    AP(VPAD[i], 128, [[MC * 129, 128], [129, 16], [1, 1]]), 1.0),
                    writes=[f"vpadones{i}"])


            def emit_w(h):
                par = h % 2
                for t_i, off in enumerate((h * 128, E + h * 128,
                                           2 * E + h * 128)):
                    wk = f"wh{par}/{t_i}"
                    P.dma("g", lambda g, off=off, t_i=t_i, par=par:
                          g.dma_start(
                              out=AP(WH[par], t_i * 128,
                                     [[KC * 384, 128], [384, 16], [1, 128]]),
                              in_=AP(wqkv, off,
                                     [[3 * E, 128], [3 * E * 128, 16],
                                      [1, 128]])),
                          writes=[wk], slot=wk)

            def qkv_units(h):
                par = h % 2
                units = []
                wkeys = [f"wh{par}/{t_i}" for t_i in range(3)]

                def u_m(m):
                    b = m % 2
                    pqk = f"pq{b}"
                    for kc in range(KC):
                        P.op("t", lambda t, m=m, kc=kc, par=par, b=b: t.matmul(
                            PQ[b][:, 0:384],
                            LNXT[:, kc * N + m * 128: kc * N + m * 128 + 128],
                            WH[par][:, kc * 384:(kc + 1) * 384],
                            start=(kc == 0), stop=(kc == KC - 1)),
                            reads=[f"lnxT/{kc}/{m}"] + wkeys, writes=[pqk])
                    P.op("v", lambda v, b=b, m=m: v.tensor_tensor(
                        T1[b][:], PQ[b][:, 0:256],
                        AP(CT, m * 128, [[MC * 128, 128], [0, 2], [1, 128]]),
                        AOT.mult), reads=[pqk, "CT"], writes=[f"t1{b}"])
                    P.op("v", lambda v, b=b, m=m: v.tensor_tensor(
                        T2[b][:],
                        AP(PQ[b], 64, [[512, 128], [128, 2], [-64, 2], [1, 64]]),
                        AP(ST, m * 128, [[MC * 128, 128], [0, 2], [1, 128]]),
                        AOT.mult), reads=[pqk, "ST"], writes=[f"t2{b}"])
                    P.op("v", lambda v, b=b, m=m, par=par: v.tensor_copy(
                        VPAD[par][:, m * 129: m * 129 + 128], PQ[b][:, 256:384]),
                        reads=[pqk], writes=[f"vpad{par}/{m}"])
                    for t_i in range(2):
                        col = 2 * m + t_i
                        P.op("v", lambda v, b=b, m=m, t_i=t_i, col=col:
                             v.scalar_tensor_tensor(
                                 QR[:, m * 256 + t_i * 128:
                                    m * 256 + t_i * 128 + 128],
                                 T1[b][:, t_i * 128:(t_i + 1) * 128], cp(C_ZERO),
                                 T2[b][:, t_i * 128:(t_i + 1) * 128],
                                 AOT.add, AOT.add,
                                 accum_out=SUMS[:, col:col + 1]),
                             reads=[f"t1{b}", f"t2{b}", "CP0"],
                             writes=[f"qr/{m}/{t_i}", f"sums/{col}"])
                        P.op("s", lambda s, m=m, t_i=t_i, col=col: s.activation(
                            P4S[col % 4][:],
                            QR[:, m * 256 + t_i * 128: m * 256 + t_i * 128 + 128],
                            ACTF.Square, accum_out=SS[:, col:col + 1]),
                            reads=[f"qr/{m}/{t_i}"],
                            writes=[f"p4s{col % 4}", f"ssq/{col}"])
                for m in range(MC):
                    units.append(lambda m=m: u_m(m))

                def u_smalls():
                    sumkeys = [f"sums/{c_}" for c_ in range(32)]
                    sskeys = [f"ssq/{c_}" for c_ in range(32)]
                    P.op("v", lambda v: v.tensor_scalar(
                        MEANS[:], SUMS[:], cp(C_I128), None, AOT.mult,
                        AOT.bypass), reads=sumkeys + ["CP1"], writes=["means"])
                    P.op("v", lambda v: v.tensor_scalar(
                        SMA[:], SS[:], cp(C_I128), None, AOT.mult, AOT.bypass),
                        reads=sskeys, writes=["sma"])
                    P.op("v", lambda v: v.tensor_tensor(
                        SMB[:], MEANS[:], MEANS[:], AOT.mult),
                        reads=["means"], writes=["smb"])
                    P.op("v", lambda v: v.tensor_tensor(
                        SMA[:], SMA[:], SMB[:], AOT.subtract),
                        reads=["sma", "smb"], writes=["sma"])
                    P.op("s", lambda s: s.activation(
                        SMB[:], SMA[:], ACTF.Sqrt, bias=cp(C_EPS)),
                        reads=["sma"], writes=["smb"])
                    P.op("v", lambda v: v.reciprocal(RQ[:], SMB[:]),
                         reads=["smb"], writes=["rq"])
                units.append(u_smalls)

                def u_norm(m):
                    for t_i in range(2):
                        col = 2 * m + t_i
                        P.op("v", lambda v, m=m, t_i=t_i, col=col:
                             v.tensor_scalar(
                                 QN[:, t_i * N + m * 128:
                                    t_i * N + m * 128 + 128],
                                 QR[:, m * 256 + t_i * 128:
                                    m * 256 + t_i * 128 + 128],
                                 MEANS[:, col:col + 1], RQ[:, col:col + 1],
                                 AOT.subtract, AOT.mult),
                             reads=[f"qr/{m}/{t_i}", "means", "rq"],
                             writes=[f"qn/{m}/{t_i}"])
                for m in range(MC):
                    units.append(lambda m=m: u_norm(m))

                def u_final():
                    qnkeys = [f"qn/{m}/{t}" for m in range(MC)
                              for t in range(2)]
                    for t_i, dst, nm in ((0, QFM, "qfm"), (1, KFM, "kfm")):
                        P.dma("y", lambda y, t_i=t_i, dst=dst, par=par:
                              y.dma_start_transpose(
                                  AP(dst[par], 0,
                                     [[N, 128], [128, 16], [1, 128]]),
                                  QN[:, t_i * N:(t_i + 1) * N]),
                              reads=qnkeys,
                              writes=[f"{nm}{par}/{m}" for m in range(MC)],
                              slot=f"qnT{t_i}")
                    qk = [f"qfm{par}/{m}" for m in range(MC)]
                    kk = [f"kfm{par}/{m}" for m in range(MC)]
                    P.op("s", lambda s, par=par: s.activation(
                        QFM[par][:], QFM[par][:], ACTF.Identity, bias=BET[:],
                        scale=GAM[:]),
                        reads=qk + ["GAM", "BET"], writes=[f"qfmF{par}"] + qk)
                    P.op("s", lambda s, par=par: s.activation(
                        KFM[par][:], KFM[par][:], ACTF.Identity, bias=BET[:],
                        scale=GAM[:]),
                        reads=kk + ["GAM", "BET"], writes=[f"kfmF{par}"] + kk)
                units.append(u_final)
                return units

            def attn_units(hh):
                par = hh % 2
                vkeys = [f"vpad{par}/{m}" for m in range(MC)] + [f"vpadones{par}"]
                units = []

                def u_s(i, j):
                    r = j - 4 * i
                    s0 = 128 * max(0, r)
                    nj = 512 - s0
                    jb = j % 6
                    pb_ = j % 2
                    psk = f"ps{pb_}"
                    P.op("t", lambda t, i=i, j=j, s0=s0, nj=nj, pb_=pb_,
                         par=par: t.matmul(
                             PS[pb_][:, 0:nj],
                             KFM[par][:, j * 128:(j + 1) * 128],
                             QFM[par][:, i * 512 + s0:(i + 1) * 512],
                             start=True, stop=True),
                         reads=[f"qfmF{par}", f"kfmF{par}"], writes=[psk])
                    j2 = j % 4
                    P.op("s", lambda s, j2=j2, pb_=pb_, nj=nj: s.activation(
                        S2B[j2][:, 0:nj], PS[pb_][:, 0:nj], ACTF.Square),
                        reads=[psk], writes=[f"s2{j2}"])
                    P.op("s", lambda s, jb=jb, j2=j2, nj=nj: s.activation(
                        S4B[jb][:, 0:nj], S2B[j2][:, 0:nj], ACTF.Square),
                        reads=[f"s2{j2}"], writes=[f"s4{jb}"])
                    if r >= 0:
                        P.op("v", lambda v, jb=jb: v.tensor_tensor(
                            S4B[jb][:, 0:128], S4B[jb][:, 0:128], TRI[:],
                            AOT.mult),
                            reads=[f"s4{jb}", "TRI"], writes=[f"s4{jb}"])

                def u_pv(i, j):
                    r = j - 4 * i
                    s0 = 128 * max(0, r)
                    jb = j % 6
                    for sub in range(max(0, r), 4):
                        P.op("t", lambda t, jb=jb, sub=sub, s0=s0, j=j,
                             par=par, i=i: t.matmul(
                                 PV[sub][:, 0:129],
                                 S4B[jb][:, 128 * sub - s0:
                                         128 * sub - s0 + 128],
                                 VPAD[par][:, j * 129: j * 129 + 129],
                                 start=(j == 0), stop=(j == 4 * i + sub)),
                             reads=[f"s4{jb}"] + vkeys, writes=[f"pv{sub}"])

                def u_epilogue(i):
                    for sub in range(4):
                        P.op("v", lambda v, sub=sub, i=i, par=par: v.tensor_copy(
                            DEN[par][:, i * 4 + sub: i * 4 + sub + 1],
                            PV[sub][:, 128:129]),
                            reads=[f"pv{sub}"], writes=[f"den{par}/{i}/{sub}"])
                    dk = [f"den{par}/{i}/{s_}" for s_ in range(4)]
                    P.op("s", lambda s, i=i, par=par: s.activation(
                        RINV[par][:, i * 4: i * 4 + 4],
                        DEN[par][:, i * 4: i * 4 + 4], ACTF.Ln),
                        reads=dk, writes=[f"lnden{par}/{i}"])
                    P.op("s", lambda s, i=i, par=par: s.activation(
                        RINV[par][:, i * 4: i * 4 + 4],
                        RINV[par][:, i * 4: i * 4 + 4], ACTF.Exp, scale=-1.0),
                        reads=[f"lnden{par}/{i}"], writes=[f"rinvA{par}/{i}"])
                    for sub in range(4):
                        blk = i * 4 + sub
                        P.op("v", lambda v, sub=sub, blk=blk, par=par, i=i:
                             v.tensor_scalar(
                                 OUTH[par][:, blk * 128:(blk + 1) * 128],
                                 PV[sub][:, 0:128],
                                 RINV[par][:, blk:blk + 1], None,
                                 AOT.mult, AOT.bypass),
                             reads=[f"pv{sub}", f"rinvA{par}/{i}"],
                             writes=[f"outh{par}/{blk}"])

                def u_spill():
                    P.dma("y", lambda y, par=par: y.dma_start_transpose(
                        AP(AFM[par], 0, [[N, 128], [128, 16], [1, 128]]),
                        AP(OUTH[par], 0, [[N, 128], [128, 16], [1, 128]])),
                        reads=[f"outh{par}/{b_}" for b_ in range(16)],
                        writes=[f"afm{par}/{b_}" for b_ in range(16)],
                        slot=f"afmT{par}")
                    akeys = [f"afm{par}/{b_}" for b_ in range(16)]
                    P.dma("g", lambda g, hh=hh, par=par: g.dma_start(
                        out=attnt_h[hh * 128:(hh + 1) * 128, :],
                        in_=AFM[par][:]),
                        reads=akeys, writes=[f"attnsp/{hh}"],
                        slot=f"attnsp{par}")

                LAG = 4
                for i in range(NB):
                    nj_ = 4 * i + 4
                    for j in range(nj_):
                        def u_sj(i=i, j=j):
                            u_s(i, j)
                            if j >= LAG:
                                u_pv(i, j - LAG)
                        units.append(u_sj)

                    def u_end(i=i, nj_=nj_):
                        for j in range(max(0, nj_ - LAG), nj_):
                            u_pv(i, j)
                        u_epilogue(i)
                    units.append(u_end)
                units.append(u_spill)
                return units

            XB0B = XB[0].bitcast(BF16)
            XB1B = XB[1].bitcast(BF16)
            P.op("v", lambda v: v.memset(ALPHACC[:], 0.0), writes=["alacc"])

            def wc_plane(kc):
                return XB0B[:, 0:E] if kc % 2 == 0 else XB1B[:, E:2 * E]

            def swt_load(kc):
                wp = kc % 2
                P.dma("g", lambda g, kc=kc: g.dma_start(
                    out=wc_plane(kc),
                    in_=AP(fw, kc * 128, [[E, 128], [E * 128, 16], [1, 128]])),
                    writes=[f"wcA{wp}", "xb0" if wp == 0 else "xb1"],
                    slot=f"wcA{wp}")

            def swt_unit(kc):
                wp = kc % 2
                wpl = wc_plane(kc)
                P.op("v", lambda v, wpl=wpl: v.tensor_scalar(
                    XB1B[:, 0:E], wpl, cp(C_ZERO), None, AOT.is_ge,
                    AOT.bypass),
                    reads=[f"wcA{wp}", "CP0"], writes=["sgA", "xb1"])
                P.op("v", lambda v: v.tensor_scalar(
                    XB1B[:, 0:E], XB1B[:, 0:E], cp(C_TWO), cp(C_ONE), AOT.mult,
                    AOT.subtract),
                    reads=["sgA", "CP4", "CP5"], writes=["sgA"])
                P.op("v", lambda v, wpl=wpl: v.tensor_reduce(
                    ALPHA[:], AP(wpl.tensor, wpl.offset,
                                 [[2 * E, 128], [128, 16], [1, 128]]),
                    AX.X, AOT.add, apply_absolute_value=True),
                    reads=[f"wcA{wp}"], writes=["alpart"])
                P.op("v", lambda v: v.tensor_tensor(
                    ALPHACC[:], ALPHACC[:], ALPHA[:], AOT.add),
                    reads=["alpart", "alacc"], writes=["alacc"])
                P.dma("y", lambda y: y.dma_start_transpose(
                    AP(XB0B, E, [[2 * E, 128], [128, 16], [1, 128]]),
                    XB1B[:, 0:E]),
                    reads=["sgA"], writes=["stgA", "xb0"], slot="swtT")
                P.dma("g", lambda g, kc=kc: g.dma_start(
                    out=swt_h[kc * 128:(kc + 1) * 128, :], in_=XB0B[:, E:2 * E]),
                    reads=["stgA"], writes=[f"swtsp/{kc}"], slot="swtsp")

            lnxt_keys = [f"lnxT/{kc}/{c}" for kc in range(KC)
                         for c in range(MC)]
            emit_w(0)
            emit_w(1)

            uq0 = qkv_units(0)
            for c in range(MC):
                b = c % 2
                xk = f"xb{b}"
                P.dma("g", lambda g, c=c, b=b: g.dma_start(
                    out=XB[b][:], in_=xh[c * 128:(c + 1) * 128, :]),
                    writes=[xk], slot=xk)
                P.op("v", lambda v, b=b: v.tensor_reduce(
                    XS[:, 0:1], XB[b][:], AX.X, AOT.add),
                    reads=[xk], writes=["xsum"])
                P.op("s", lambda s, b=b: s.activation(
                    SCR[:], XB[b][:], ACTF.Square, accum_out=XS[:, 1:2]),
                    reads=[xk], writes=["xss", "scrA"])
                ln_smalls(XS, "x", "xsum", "xss")
                P.op("v", lambda v: v.tensor_scalar(
                    XS[:, 7:8], XS[:, 0:1], cp(C_NI2048), None,
                    AOT.mult, AOT.bypass),
                    reads=["xsum", "CP6"], writes=["xnm"])
                P.op("v", lambda v: v.tensor_tensor(
                    XS[:, 7:8], XS[:, 7:8], XS[:, 6:7], AOT.mult),
                    reads=["xnm", "xr"], writes=["xnm"])
                P.op("s", lambda s, b=b: s.activation(
                    LNC[b][:], XB[b][:], ACTF.Identity, bias=XS[:, 7:8],
                    scale=XS[:, 6:7]),
                    reads=[xk, "xr", "xnm"], writes=[f"lnc{b}"])
                P.dma("y", lambda y, c=c, b=b: y.dma_start_transpose(
                    AP(LNXT, c * 128, [[KC * N, 128], [N, 16], [1, 128]]),
                    AP(LNC[b], 0, [[E, 128], [128, 16], [1, 128]])),
                    reads=[f"lnc{b}"],
                    writes=[f"lnxT/{kc}/{c}" for kc in range(KC)],
                    slot=f"lnxTd{b}")
                uq0[c]()
            for th in uq0[MC:]:
                th()
            emit_w(2)
            swt_load(0)

            for h in range(1, H + 1):
                uq = qkv_units(h) if h < H else []
                ua = attn_units(h - 1) if h >= 1 else []
                ia = 0
                SHIFT = 3
                for iq, th in enumerate(uq):
                    th()
                    want = int(round((iq + 1 - SHIFT) * len(ua)
                                     / max(1, len(uq) - SHIFT)))
                    while ia < min(max(0, want), len(ua)):
                        ua[ia]()
                        ia += 1
                while ia < len(ua):
                    ua[ia]()
                    ia += 1
                if h + 2 < H:
                    emit_w(h + 2)
                if h == 0:
                    swt_load(0)
                elif h < H:
                    swt_unit(h - 1)
                    swt_load(h)
                else:
                    swt_unit(H - 1)
                if h == H - 1:
                    P.dma("g", lambda g: g.dma_start(
                        out=AP(LNXT, 0, [[KC * N, 128], [N, 16], [1, N]]),
                        in_=AP(wout, 0, [[E, 128], [E * 128, 16], [1, E]])),
                        writes=lnxt_keys + ["woutP"], slot="woutP")
            P.op("v", lambda v: v.tensor_scalar(
                ALPHA[:], ALPHACC[:], cp(C_I2048), None, AOT.mult, AOT.bypass),
                reads=["alacc", "CP2"], writes=["alphaF", "alpart"])

        P.fence()

        with contextlib.ExitStack() as sB:
            TB = sB.enter_context
            ATK = [TB(nc.sbuf_tensor(f"ATK{i}", [128, KC * 512], BF16))
                   for i in range(2)]
            XB2 = [TB(nc.sbuf_tensor(f"XB2{i}", [128, E], F32)) for i in range(4)]
            APJT = [TB(nc.sbuf_tensor(f"APJT{i}", [128, 512], BF16))
                    for i in range(4)]
            APJ = [TB(nc.sbuf_tensor(f"APJ{i}", [128, 4 * E], BF16))
                   for i in range(2)]
            SCB = TB(nc.sbuf_tensor("SCB", [128, E], BF16))
            O1 = [TB(nc.sbuf_tensor(f"O1{i}", [128, E], F32)) for i in range(2)]
            LN2C = [TB(nc.sbuf_tensor(f"LN2C{i}", [128, E], BF16))
                    for i in range(2)]
            L2F = [TB(nc.sbuf_tensor(f"L2F{i}", [128, KC * 128], BF16))
                   for i in range(2)]
            BS = TB(nc.sbuf_tensor("BS", [128, 32], F32))
            PB = [TB(nc.psum_tensor(f"PB{i}", [128, 512], F32)) for i in range(4)]

            P.barrier("g", ["g:attnsp0", "g:attnsp1"], writes=["attn_done"])

            def load_atk(tb):
                tbb = tb % 2
                P.dma("g", lambda g, tb=tb, tbb=tbb: g.dma_start(
                    out=ATK[tbb][:],
                    in_=AP(attnt_h, tb * 512,
                           [[N, 128], [N * 128, 16], [1, 512]])),
                    reads=["attn_done"], writes=[f"atk{tbb}"], slot=f"atk{tbb}")

            def b_s_unit(tb, s_):
                tbb = tb % 2
                pb = s_ % 4
                for kc in range(KC):
                    P.op("t", lambda t, s_=s_, kc=kc, pb=pb, tbb=tbb: t.matmul(
                        PB[pb][:],
                        LNXT[:, kc * N + s_ * 128: kc * N + s_ * 128 + 128],
                        ATK[tbb][:, kc * 512:(kc + 1) * 512],
                        start=(kc == 0), stop=(kc == KC - 1)),
                        reads=["woutP", f"atk{tbb}"], writes=[f"pb{pb}"])
                P.op("s", lambda s, pb=pb: s.activation(
                    APJT[pb][:], PB[pb][:], ACTF.Identity),
                    reads=[f"pb{pb}"], writes=[f"apjT{pb}"])
                P.dma("y", lambda y, s_=s_, pb=pb, tbb=tbb:
                      y.dma_start_transpose(
                          AP(APJ[tbb], s_ * 128,
                             [[4 * E, 128], [E, 4], [1, 128]]),
                          AP(APJT[pb], 0, [[512, 128], [128, 4], [1, 128]])),
                      reads=[f"apjT{pb}"],
                      writes=[f"apj{tbb}/{cl}/{s_}" for cl in range(4)],
                      slot=f"apjTd{pb}")

            def b_cl_unit(tb, cl):
                tbb = tb % 2
                c = tb * 4 + cl
                cb = c % 2
                b0 = cl * 8
                kp = f"B{cl}"
                apk = [f"apj{tbb}/{cl}/{s_}" for s_ in range(16)]
                P.op("v", lambda v, cl=cl, tbb=tbb, b0=b0: v.tensor_reduce(
                    BS[:, b0:b0 + 1], APJ[tbb][:, cl * E:(cl + 1) * E], AX.X,
                    AOT.add), reads=apk, writes=[kp + "sum"])
                P.op("s", lambda s, cl=cl, tbb=tbb, b0=b0: s.activation(
                    SCB[:], APJ[tbb][:, cl * E:(cl + 1) * E], ACTF.Square,
                    accum_out=BS[:, b0 + 1:b0 + 2]),
                    reads=apk, writes=[kp + "ss", "scb"])
                ln_smalls(BS, kp, kp + "sum", kp + "ss", b0=b0)
                P.op("v", lambda v, cl=cl, tbb=tbb, cb=cb, b0=b0:
                     v.tensor_scalar(
                         O1[cb][:], APJ[tbb][:, cl * E:(cl + 1) * E],
                         BS[:, b0 + 2:b0 + 3], BS[:, b0 + 6:b0 + 7],
                         AOT.subtract, AOT.mult),
                     reads=apk + [kp + "m", kp + "r"], writes=[f"o1{cb}"])
                P.op("v", lambda v, cb=cb, c=c, b0=b0: v.scalar_tensor_tensor(
                    O1[cb][:], O1[cb][:], cp(C_ZERO), XB2[c % 4][:], AOT.add,
                    AOT.add, accum_out=BS[:, b0:b0 + 1]),
                    reads=[f"o1{cb}", f"xb2{c % 4}", "CP0"],
                    writes=[f"o1{cb}", kp + "sum"])
                P.dma("g", lambda g, c=c, cb=cb: g.dma_start(
                    out=out1_h[c * 128:(c + 1) * 128, :], in_=O1[cb][:]),
                    reads=[f"o1{cb}"], writes=[f"o1sp/{c}"], slot=f"o1sp{cb}")
                P.op("s", lambda s, cb=cb, b0=b0: s.activation(
                    SCB[:], O1[cb][:], ACTF.Square,
                    accum_out=BS[:, b0 + 1:b0 + 2]),
                    reads=[f"o1{cb}"], writes=[kp + "ss", "scb"])
                ln_smalls(BS, kp + "L2", kp + "sum", kp + "ss", b0=b0)
                P.op("v", lambda v, cb=cb, b0=b0: v.tensor_scalar(
                    LN2C[cb][:], O1[cb][:], BS[:, b0 + 2:b0 + 3],
                    BS[:, b0 + 6:b0 + 7], AOT.subtract, AOT.mult),
                    reads=[f"o1{cb}", kp + "L2m", kp + "L2r"],
                    writes=[f"ln2c{cb}"])
                P.dma("y", lambda y, cb=cb: y.dma_start_transpose(
                    AP(L2F[cb], 0, [[KC * 128, 128], [128, 16], [1, 128]]),
                    AP(LN2C[cb], 0, [[E, 128], [128, 16], [1, 128]])),
                    reads=[f"ln2c{cb}"],
                    writes=[f"l2f{cb}/{kc}" for kc in range(KC)],
                    slot=f"l2fT{cb}")
                P.dma("g", lambda g, c=c, cb=cb: g.dma_start(
                    out=AP(lnx2t_h, c * 128,
                           [[N, 128], [N * 128, 16], [1, 128]]),
                    in_=L2F[cb][:]),
                    reads=[f"l2f{cb}/{kc}" for kc in range(KC)],
                    writes=[f"l2sp/{c}"], slot=f"l2sp{cb}")

            def load_xb2(c):
                P.dma("g", lambda g, c=c: g.dma_start(
                    out=XB2[c % 4][:], in_=xh[c * 128:(c + 1) * 128, :]),
                    writes=[f"xb2{c % 4}"], slot=f"xb2{c % 4}")

            load_atk(0)
            for cl in range(4):
                load_xb2(cl)
            for tb in range(4):
                if tb + 1 < 4:
                    load_atk(tb + 1)
                for s_ in range(16):
                    b_s_unit(tb, s_)
                    if tb >= 1 and s_ % 4 == 3:
                        cl = s_ // 4
                        b_cl_unit(tb - 1, cl)
                        load_xb2(tb * 4 + cl)
            for cl in range(4):
                b_cl_unit(3, cl)

            P.dma("g", lambda g: g.dma_start(
                out=AP(LNXT, 0, [[KC * N, 128], [N, 16], [1, N]]),
                in_=AP(swt_h, 0, [[N, 128], [N * 128, 16], [1, N]])),
                reads=[f"swtsp/{kc}" for kc in range(KC)],
                writes=["woutP", "swtP"], slot="swtP")

        P.fence()

        with contextlib.ExitStack() as sC:
            TC = sC.enter_context
            LX = [TC(nc.sbuf_tensor(f"LX{i}", [128, KC * 512], BF16))
                  for i in range(2)]
            YTT = [TC(nc.sbuf_tensor(f"YTT{i}", [128, 512], BF16))
                   for i in range(4)]
            YTM = [TC(nc.sbuf_tensor(f"YTM{i}", [128, 4 * E], BF16))
                   for i in range(2)]
            O1R = [TC(nc.sbuf_tensor(f"O1R{i}", [128, E], F32)) for i in range(4)]
            OC = [TC(nc.sbuf_tensor(f"OC{i}", [128, E], F32)) for i in range(2)]
            SCC = TC(nc.sbuf_tensor("SCC", [128, E], BF16))
            CS = TC(nc.sbuf_tensor("CS", [128, 32], F32))
            PC = [TC(nc.psum_tensor(f"PC{i}", [128, 512], F32)) for i in range(4)]

            P.barrier("g", ["g:l2sp0", "g:l2sp1"], writes=["l2_done"])
            P.barrier("g", ["g:o1sp0", "g:o1sp1"], writes=["o1_done"])

            def load_lx(tb):
                tbb = tb % 2
                P.dma("g", lambda g, tb=tb, tbb=tbb: g.dma_start(
                    out=LX[tbb][:],
                    in_=AP(lnx2t_h, tb * 512,
                           [[N, 128], [N * 128, 16], [1, 512]])),
                    reads=["l2_done"], writes=[f"lx{tbb}"], slot=f"lx{tbb}")

            def c_s_unit(tb, s_):
                tbb = tb % 2
                pb = s_ % 4
                for kc in range(KC):
                    P.op("t", lambda t, s_=s_, kc=kc, pb=pb, tbb=tbb: t.matmul(
                        PC[pb][:],
                        LNXT[:, kc * N + s_ * 128: kc * N + s_ * 128 + 128],
                        LX[tbb][:, kc * 512:(kc + 1) * 512],
                        start=(kc == 0), stop=(kc == KC - 1)),
                        reads=["swtP", f"lx{tbb}"], writes=[f"pc{pb}"])
                P.op("s", lambda s, s_=s_, pb=pb: s.activation(
                    YTT[pb][:], PC[pb][:], ACTF.Identity,
                    bias=BIASF[:, s_:s_ + 1], scale=ALPHA[:, s_:s_ + 1]),
                    reads=[f"pc{pb}", "alphaF", "BIASF"], writes=[f"ytt{pb}"])
                P.dma("y", lambda y, s_=s_, pb=pb, tbb=tbb:
                      y.dma_start_transpose(
                          AP(YTM[tbb], s_ * 128,
                             [[4 * E, 128], [E, 4], [1, 128]]),
                          AP(YTT[pb], 0, [[512, 128], [128, 4], [1, 128]])),
                      reads=[f"ytt{pb}"],
                      writes=[f"ytm{tbb}/{cl}/{s_}" for cl in range(4)],
                      slot=f"ytmTd{pb}")

            def c_cl_unit(tb, cl):
                tbb = tb % 2
                c = tb * 4 + cl
                cb = c % 2
                b0 = cl * 8
                kp = f"C{cl}"
                ymk = [f"ytm{tbb}/{cl}/{s_}" for s_ in range(16)]
                P.op("v", lambda v, cl=cl, tbb=tbb, b0=b0: v.tensor_reduce(
                    CS[:, b0:b0 + 1], YTM[tbb][:, cl * E:(cl + 1) * E], AX.X,
                    AOT.add), reads=ymk, writes=[kp + "sum"])
                P.op("s", lambda s, cl=cl, tbb=tbb, b0=b0: s.activation(
                    SCC[:], YTM[tbb][:, cl * E:(cl + 1) * E], ACTF.Square,
                    accum_out=CS[:, b0 + 1:b0 + 2]),
                    reads=ymk, writes=[kp + "ss", "sccS"])
                ln_smalls(CS, kp, kp + "sum", kp + "ss", b0=b0)
                P.op("v", lambda v, cl=cl, tbb=tbb, cb=cb, b0=b0:
                     v.tensor_scalar(
                         OC[cb][:], YTM[tbb][:, cl * E:(cl + 1) * E],
                         CS[:, b0 + 2:b0 + 3], CS[:, b0 + 6:b0 + 7],
                         AOT.subtract, AOT.mult),
                     reads=ymk + [kp + "m", kp + "r"], writes=[f"oc{cb}"])
                P.op("v", lambda v, cb=cb, c=c: v.tensor_tensor(
                    OC[cb][:], OC[cb][:], O1R[c % 4][:], AOT.add),
                    reads=[f"oc{cb}", f"o1r{c % 4}"], writes=[f"oc{cb}"])
                P.dma("g", lambda g, c=c, cb=cb: g.dma_start(
                    out=out_h[c * 128:(c + 1) * 128, :], in_=OC[cb][:]),
                    reads=[f"oc{cb}"], writes=[f"outsp/{c}"],
                    slot=f"outsp{cb}")

            def load_o1r(c):
                P.dma("g", lambda g, c=c: g.dma_start(
                    out=O1R[c % 4][:], in_=out1_h[c * 128:(c + 1) * 128, :]),
                    reads=["o1_done"], writes=[f"o1r{c % 4}"],
                    slot=f"o1r{c % 4}")

            load_lx(0)
            for cl in range(4):
                load_o1r(cl)
            for tb in range(4):
                if tb + 1 < 4:
                    load_lx(tb + 1)
                for s_ in range(16):
                    c_s_unit(tb, s_)
                    if tb >= 1 and s_ % 4 == 3:
                        cl = s_ // 4
                        c_cl_unit(tb - 1, cl)
                        load_o1r(tb * 4 + cl)
            for cl in range(4):
                c_cl_unit(3, cl)

            P.barrier("g", ["g:outsp0", "g:outsp1"], writes=["all_done"])

            P.finalize_marks()
            with contextlib.ExitStack() as se:
                SE = se.enter_context
                sems = {e: SE(nc.semaphore(f"sem_{e}")) for e in ("t", "v", "s")}
                slot_sems = {}
                for i_, sk in enumerate(P.slots):
                    slot_sems[sk] = SE(nc.semaphore(f"sd{i_}"))
                block = SE(nc.Block())

                @block.tensor
                def _(t):
                    P.emit_one("t", t, sems, slot_sems)

                @block.vector
                def _(v):
                    P.emit_one("v", v, sems, slot_sems)

                @block.scalar
                def _(s):
                    P.emit_one("s", s, sems, slot_sems)

                @block.gpsimd
                def _(g):
                    P.emit_one("g", g, sems, slot_sems)

                @block.sync
                def _(y):
                    P.emit_one("y", y, sems, slot_sems)

    return nc


_CACHE = {}


def kernel(L0_attn_x, L0_attn_w_qkv, L0_attn_w_out, L0_attn_ln_scale,
           L0_attn_ln_bias, L0_ffn_weight, L0_ffn_bias):
    x = np.ascontiguousarray(np.asarray(L0_attn_x, dtype=np.float32))
    consts = host_consts()
    if "nc" not in _CACHE:
        _CACHE["nc"] = build_nc()
    nc = _CACHE["nc"]
    base = {
        "wqkv": np.ascontiguousarray(np.asarray(L0_attn_w_qkv, np.float32)),
        "wout": np.ascontiguousarray(np.asarray(L0_attn_w_out, np.float32)),
        "gam": np.ascontiguousarray(np.asarray(L0_attn_ln_scale, np.float32)),
        "bet": np.ascontiguousarray(np.asarray(L0_attn_ln_bias, np.float32)),
        "fw": np.ascontiguousarray(np.asarray(L0_ffn_weight, np.float32)),
        "fb": np.ascontiguousarray(np.asarray(L0_ffn_bias, np.float32)),
        "ctab": consts["ctab"], "stab": consts["stab"], "tri": consts["tri"],
    }
    in_maps = [dict(base, xb=np.ascontiguousarray(x[b])) for b in range(8)]
    res = run_bass_kernel_spmd(nc, in_maps, core_ids=list(range(8)))
    return np.stack([np.asarray(res.results[b]["out"], np.float32)
                     for b in range(8)], axis=0)


# version 18
# speedup vs baseline: 1.1696x; 1.1696x over previous
import contextlib
import numpy as np
import ml_dtypes
import concourse.bass as bass
import concourse.mybir as mybir
from concourse.ap import AP
from concourse.bass_utils import run_bass_kernel_spmd

F32 = mybir.dt.float32
BF16 = mybir.dt.bfloat16
AOT = mybir.AluOpType
ACTF = mybir.ActivationFunctionType
AX = mybir.AxisListType

N = 2048
E = 2048
H = 16
D = 128
MC = 16
KC = 16
NB = 4
EPS = 1e-5


class Op:
    __slots__ = ("eng", "fn", "deps", "inc", "slot", "use", "idx", "cum",
                 "dma_waits", "drain_upto")

    def __init__(self, eng, fn, idx):
        self.eng = eng
        self.fn = fn
        self.deps = []
        self.inc = False
        self.slot = None
        self.use = 0
        self.idx = idx
        self.cum = 0
        self.dma_waits = ()
        self.drain_upto = -1


class Prog:
    ENGS = ("t", "v", "s", "g", "y")

    def __init__(self):
        self.streams = {e: [] for e in self.ENGS}
        self.writer = {}
        self.readers = {}
        self.slots = {}

    def _dep(self, op, dep):
        if dep is op or dep is None:
            return
        if dep.slot is None and dep.eng == op.eng and (
                dep.eng in ("t", "g", "y") or dep.fn is None):
            return
        if dep.slot is not None:
            val = 16 * self.slots[dep.slot]
            for i, (d, v) in enumerate(op.deps):
                if d is dep:
                    if val > v:
                        op.deps[i] = (dep, val)
                    return
            op.deps.append((dep, val))
            return
        for d, _ in op.deps:
            if d is dep:
                return
        op.deps.append((dep, None))

    def _record(self, eng, fn, reads, writes, slot, dma_waits):
        op = Op(eng, fn, len(self.streams[eng]))
        if slot is not None:
            op.slot = slot
        op.dma_waits = tuple(dma_waits)
        for k in reads:
            w = self.writer.get(k)
            if w is not None:
                self._dep(op, w)
        for k in writes:
            rd = self.readers.get(k)
            if rd:
                for r in rd.values():
                    self._dep(op, r)
            w = self.writer.get(k)
            if w is not None:
                self._dep(op, w)
        if slot is not None:
            self.slots[slot] = self.slots.get(slot, 0) + 1
            op.use = self.slots[slot]
        for k in reads:
            self.readers.setdefault(k, {})[eng] = op
        for k in writes:
            self.writer[k] = op
            self.readers[k] = {}
        self.streams[eng].append(op)
        return op

    def op(self, eng, fn, reads=(), writes=()):
        assert eng in ("t", "v", "s")
        return self._record(eng, fn, reads, writes, None, ())

    def dma(self, ring, fn, reads=(), writes=(), slot=None):
        assert ring in ("g", "y")
        if slot is None:
            slot = "dma:" + (writes[0] if writes else "anon")
        return self._record(ring, fn, reads, writes, ring + ":" + slot, ())

    def barrier(self, eng, slot_waits, writes=()):
        waits = [(k, 16 * self.slots.get(k, 0)) for k in slot_waits]
        return self._record(eng, None, (), writes, None, waits)

    def fence(self):
        lasts = {e: (self.streams[e][-1] if self.streams[e] else None)
                 for e in self.ENGS}
        slot_totals = [(k, 16 * v) for k, v in self.slots.items()]
        for e in self.ENGS:
            op = self._record(e, None, (), (), None, slot_totals)
            for e2, lo in lasts.items():
                if lo is not None and e2 != e and lo.slot is None:
                    self._dep(op, lo)

    def finalize_marks(self):
        for e in self.ENGS:
            for op in self.streams[e]:
                for dep, _ in op.deps:
                    dep.inc = True
        for e in ("t", "v", "s"):
            c = 0
            for op in self.streams[e]:
                if op.inc:
                    c += 1
                op.cum = c

    def emit_one(self, e, eng, sems, slot_sems):
        waited = {}
        drained = -1
        for i, op in enumerate(self.streams[e]):
            for sk, val in op.dma_waits:
                hs = slot_sems[sk]
                if waited.get(id(hs), -1) < val:
                    eng.wait_ge(hs, val)
                    waited[id(hs)] = val
            for dep, dval in op.deps:
                if dep.slot is not None:
                    hs = slot_sems[dep.slot]
                    val = dval
                else:
                    hs = sems[dep.eng]
                    val = dep.cum
                if waited.get(id(hs), -1) < val:
                    eng.wait_ge(hs, val)
                    waited[id(hs)] = val
            if op.drain_upto > drained:
                eng.drain()
                drained = i - 1
            if op.fn is None:
                if op.inc and e in sems:
                    eng.nop().then_inc(sems[e], 1)
                continue
            inst = op.fn(eng)
            if op.slot is not None:
                inst.then_inc(slot_sems[op.slot], 16)
            elif op.inc:
                inst.then_inc(sems[e], 1)


def host_consts():
    inv = 1.0 / (10000.0 ** (np.arange(0, D, 2, dtype=np.float32) / D))
    t = np.arange(N, dtype=np.float32)[:, None] * inv[None, :]
    cos = np.cos(t).astype(np.float32)
    sin = np.sin(t).astype(np.float32)
    return {
        "ctab": np.ascontiguousarray(
            np.concatenate([cos, cos], 1).astype(ml_dtypes.bfloat16)),
        "stab": np.ascontiguousarray(
            np.concatenate([-sin, sin], 1).astype(ml_dtypes.bfloat16)),
        "tri": np.triu(np.ones((128, 128), np.float32)).astype(ml_dtypes.bfloat16),
    }


def build_nc():
    nc = bass.Bass("TRN2", target_bir_lowering=False)

    xh = nc.dram_tensor("xb", [N, E], F32, kind="ExternalInput")
    wqkv = nc.dram_tensor("wqkv", [E, 3 * E], F32, kind="ExternalInput")
    wout = nc.dram_tensor("wout", [E, E], F32, kind="ExternalInput")
    gam = nc.dram_tensor("gam", [D], F32, kind="ExternalInput")
    bet = nc.dram_tensor("bet", [D], F32, kind="ExternalInput")
    fw = nc.dram_tensor("fw", [E, E], F32, kind="ExternalInput")
    fb = nc.dram_tensor("fb", [E], F32, kind="ExternalInput")
    ctabh = nc.dram_tensor("ctab", [N, D], BF16, kind="ExternalInput")
    stabh = nc.dram_tensor("stab", [N, D], BF16, kind="ExternalInput")
    trih = nc.dram_tensor("tri", [128, 128], BF16, kind="ExternalInput")
    out_h = nc.dram_tensor("out", [N, E], F32, kind="ExternalOutput")
    attnt_h = nc.dram_tensor("attnt_sc", [E, N], BF16, kind="Internal")
    out1_h = nc.dram_tensor("out1_sc", [N, E], F32, kind="Internal")
    lnx2t_h = nc.dram_tensor("lnx2t_sc", [E, N], BF16, kind="Internal")
    swt_h = nc.dram_tensor("swt_sc", [E, N], BF16, kind="Internal")

    P = Prog()
    with contextlib.ExitStack() as top:
        TE = top.enter_context
        LNXT = TE(nc.sbuf_tensor("LNXT", [128, KC * N], BF16))
        GAM = TE(nc.sbuf_tensor("GAM", [128, 1], F32))
        BET = TE(nc.sbuf_tensor("BET", [128, 1], F32))
        BIASF = TE(nc.sbuf_tensor("BIASF", [128, 16], F32))
        CP = TE(nc.sbuf_tensor("CP", [128, 8], F32))
        ALPHA = TE(nc.sbuf_tensor("ALPHA", [128, 16], F32))
        ALPHACC = TE(nc.sbuf_tensor("ALPHACC", [128, 16], F32))

        C_ZERO, C_I128, C_I2048, C_EPS, C_TWO, C_ONE, C_NI2048 = \
            0, 1, 2, 3, 4, 5, 6

        def cp(col):
            return CP[:, col:col + 1]

        def ld(dst_ap, src_ap, key, slow=False):
            P.dma("g", lambda g, d=dst_ap, s=src_ap, sl=slow: g.dma_start(
                out=d, in_=s, allow_slow_non_contiguous=sl),
                writes=[key], slot=key)

        ld(GAM[:], AP(gam, 0, [[1, 128], [1, 1]]), "GAM", slow=True)
        ld(BET[:], AP(bet, 0, [[1, 128], [1, 1]]), "BET", slow=True)
        ld(BIASF[:], AP(fb, 0, [[1, 128], [128, 16]]), "BIASF", slow=True)
        for col, val in [(C_ZERO, 0.0), (C_I128, 1.0 / 128.0),
                         (C_I2048, 1.0 / 2048.0), (C_EPS, EPS),
                         (C_TWO, 2.0), (C_ONE, 1.0),
                         (C_NI2048, -1.0 / 2048.0)]:
            P.op("v", lambda v, c=col, vv=val: v.memset(CP[:, c:c + 1], vv),
                 writes=[f"CP{col}"])

        def ln_smalls(S, kp, sumk, ssk, b0=0):
            P.op("v", lambda v: v.tensor_scalar(
                S[:, b0 + 2:b0 + 3], S[:, b0:b0 + 1], cp(C_I2048), None,
                AOT.mult, AOT.bypass), reads=[sumk, "CP2"], writes=[kp + "m"])
            P.op("v", lambda v: v.tensor_scalar(
                S[:, b0 + 3:b0 + 4], S[:, b0 + 1:b0 + 2], cp(C_I2048), None,
                AOT.mult, AOT.bypass), reads=[ssk], writes=[kp + "e"])
            P.op("v", lambda v: v.tensor_tensor(
                S[:, b0 + 4:b0 + 5], S[:, b0 + 2:b0 + 3], S[:, b0 + 2:b0 + 3],
                AOT.mult), reads=[kp + "m"], writes=[kp + "q"])
            P.op("v", lambda v: v.tensor_tensor(
                S[:, b0 + 4:b0 + 5], S[:, b0 + 3:b0 + 4], S[:, b0 + 4:b0 + 5],
                AOT.subtract), reads=[kp + "e", kp + "q"], writes=[kp + "q"])
            P.op("s", lambda s: s.activation(
                S[:, b0 + 5:b0 + 6], S[:, b0 + 4:b0 + 5], ACTF.Sqrt,
                bias=cp(C_EPS)), reads=[kp + "q", "CP3"], writes=[kp + "sd"])
            P.op("v", lambda v: v.reciprocal(
                S[:, b0 + 6:b0 + 7], S[:, b0 + 5:b0 + 6]),
                 reads=[kp + "sd"], writes=[kp + "r"])

        with contextlib.ExitStack() as sA:
            TA = sA.enter_context
            CT = TA(nc.sbuf_tensor("CT", [128, MC * 128], BF16))
            ST = TA(nc.sbuf_tensor("ST", [128, MC * 128], BF16))
            TRI = TA(nc.sbuf_tensor("TRI", [128, 128], BF16))
            DEN = [TA(nc.sbuf_tensor(f"DEN{i}", [128, 16], F32)) for i in range(2)]
            RINV = [TA(nc.sbuf_tensor(f"RINV{i}", [128, 16], F32)) for i in range(2)]
            OUTH = [TA(nc.sbuf_tensor(f"OUTH{i}", [128, N], BF16)) for i in range(2)]
            AFM = [TA(nc.sbuf_tensor(f"AFM{i}", [128, N], BF16)) for i in range(2)]
            ld(CT[:], AP(ctabh, 0, [[128, 128], [128 * 128, 16], [1, 128]]), "CT")
            ld(ST[:], AP(stabh, 0, [[128, 128], [128 * 128, 16], [1, 128]]), "ST")
            ld(TRI[:], trih.ap(), "TRI")
            XB = [TA(nc.sbuf_tensor(f"XB{i}", [128, E], F32)) for i in range(2)]
            LNC = [TA(nc.sbuf_tensor(f"LNC{i}", [128, E], BF16)) for i in range(2)]
            XS = TA(nc.sbuf_tensor("XS", [128, 8], F32))
            SCR = TA(nc.sbuf_tensor("SCR", [128, E], BF16))
            WH = [TA(nc.sbuf_tensor(f"WH{i}", [128, KC * 384], BF16)) for i in range(2)]
            QR = TA(nc.sbuf_tensor("QR", [128, MC * 256], F32))
            T1 = [TA(nc.sbuf_tensor(f"T1{i}", [128, 256], F32)) for i in range(2)]
            T2 = [TA(nc.sbuf_tensor(f"T2{i}", [128, 256], F32)) for i in range(2)]
            SUMS = TA(nc.sbuf_tensor("SUMS", [128, 32], F32))
            SS = TA(nc.sbuf_tensor("SS", [128, 32], F32))
            MEANS = TA(nc.sbuf_tensor("MEANS", [128, 32], F32))
            RQ = TA(nc.sbuf_tensor("RQ", [128, 32], F32))
            SMA = TA(nc.sbuf_tensor("SMA", [128, 32], F32))
            SMB = TA(nc.sbuf_tensor("SMB", [128, 32], F32))
            QN = TA(nc.sbuf_tensor("QN", [128, MC * 256], BF16))
            P4S = [TA(nc.sbuf_tensor(f"P4S{i}", [128, 128], F32)) for i in range(4)]
            QFM = [TA(nc.sbuf_tensor(f"QFM{i}", [128, N], BF16)) for i in range(2)]
            KFM = [TA(nc.sbuf_tensor(f"KFM{i}", [128, N], BF16)) for i in range(2)]
            VPAD = [TA(nc.sbuf_tensor(f"VPAD{i}", [128, MC * 129], BF16))
                    for i in range(2)]
            S2B = [TA(nc.sbuf_tensor(f"S2B{i}", [128, 512], BF16)) for i in range(4)]
            S4B = [TA(nc.sbuf_tensor(f"S4B{i}", [128, 512], BF16)) for i in range(6)]
            PQ = [TA(nc.psum_tensor(f"PQ{i}", [128, 512], F32)) for i in range(2)]
            PS = [TA(nc.psum_tensor(f"PS{i}", [128, 512], F32)) for i in range(2)]
            PV = [TA(nc.psum_tensor(f"PV{i}", [128, 512], F32)) for i in range(4)]

            for i in range(2):
                P.op("v", lambda v, i=i: v.memset(
                    AP(VPAD[i], 128, [[MC * 129, 128], [129, 16], [1, 1]]), 1.0),
                    writes=[f"vpadones{i}"])


            def emit_w(h):
                par = h % 2
                for t_i, off in enumerate((h * 128, E + h * 128,
                                           2 * E + h * 128)):
                    wk = f"wh{par}/{t_i}"
                    P.dma("g", lambda g, off=off, t_i=t_i, par=par:
                          g.dma_start(
                              out=AP(WH[par], t_i * 128,
                                     [[KC * 384, 128], [384, 16], [1, 128]]),
                              in_=AP(wqkv, off,
                                     [[3 * E, 128], [3 * E * 128, 16],
                                      [1, 128]])),
                          writes=[wk], slot=wk)

            def qkv_units(h):
                par = h % 2
                units = []
                wkeys = [f"wh{par}/{t_i}" for t_i in range(3)]

                def u_m(m):
                    b = m % 2
                    pqk = f"pq{b}"
                    for kc in range(KC):
                        P.op("t", lambda t, m=m, kc=kc, par=par, b=b: t.matmul(
                            PQ[b][:, 0:384],
                            LNXT[:, kc * N + m * 128: kc * N + m * 128 + 128],
                            WH[par][:, kc * 384:(kc + 1) * 384],
                            start=(kc == 0), stop=(kc == KC - 1)),
                            reads=[f"lnxT/{kc}/{m}"] + wkeys, writes=[pqk])
                    P.op("v", lambda v, b=b, m=m: v.tensor_tensor(
                        T1[b][:], PQ[b][:, 0:256],
                        AP(CT, m * 128, [[MC * 128, 128], [0, 2], [1, 128]]),
                        AOT.mult), reads=[pqk, "CT"], writes=[f"t1{b}"])
                    P.op("v", lambda v, b=b, m=m: v.tensor_tensor(
                        T2[b][:],
                        AP(PQ[b], 64, [[512, 128], [128, 2], [-64, 2], [1, 64]]),
                        AP(ST, m * 128, [[MC * 128, 128], [0, 2], [1, 128]]),
                        AOT.mult), reads=[pqk, "ST"], writes=[f"t2{b}"])
                    P.op("v", lambda v, b=b, m=m, par=par: v.tensor_copy(
                        VPAD[par][:, m * 129: m * 129 + 128], PQ[b][:, 256:384]),
                        reads=[pqk], writes=[f"vpad{par}/{m}"])
                    for t_i in range(2):
                        col = 2 * m + t_i
                        P.op("v", lambda v, b=b, m=m, t_i=t_i, col=col:
                             v.scalar_tensor_tensor(
                                 QR[:, m * 256 + t_i * 128:
                                    m * 256 + t_i * 128 + 128],
                                 T1[b][:, t_i * 128:(t_i + 1) * 128], cp(C_ZERO),
                                 T2[b][:, t_i * 128:(t_i + 1) * 128],
                                 AOT.add, AOT.add,
                                 accum_out=SUMS[:, col:col + 1]),
                             reads=[f"t1{b}", f"t2{b}", "CP0"],
                             writes=[f"qr/{m}/{t_i}", f"sums/{col}"])
                        P.op("s", lambda s, m=m, t_i=t_i, col=col: s.activation(
                            P4S[col % 4][:],
                            QR[:, m * 256 + t_i * 128: m * 256 + t_i * 128 + 128],
                            ACTF.Square, accum_out=SS[:, col:col + 1]),
                            reads=[f"qr/{m}/{t_i}"],
                            writes=[f"p4s{col % 4}", f"ssq/{col}"])
                for m in range(MC):
                    units.append(lambda m=m: u_m(m))

                def u_smalls():
                    sumkeys = [f"sums/{c_}" for c_ in range(32)]
                    sskeys = [f"ssq/{c_}" for c_ in range(32)]
                    P.op("v", lambda v: v.tensor_scalar(
                        MEANS[:], SUMS[:], cp(C_I128), None, AOT.mult,
                        AOT.bypass), reads=sumkeys + ["CP1"], writes=["means"])
                    P.op("v", lambda v: v.tensor_scalar(
                        SMA[:], SS[:], cp(C_I128), None, AOT.mult, AOT.bypass),
                        reads=sskeys, writes=["sma"])
                    P.op("v", lambda v: v.tensor_tensor(
                        SMB[:], MEANS[:], MEANS[:], AOT.mult),
                        reads=["means"], writes=["smb"])
                    P.op("v", lambda v: v.tensor_tensor(
                        SMA[:], SMA[:], SMB[:], AOT.subtract),
                        reads=["sma", "smb"], writes=["sma"])
                    P.op("s", lambda s: s.activation(
                        SMB[:], SMA[:], ACTF.Sqrt, bias=cp(C_EPS)),
                        reads=["sma"], writes=["smb"])
                    P.op("v", lambda v: v.reciprocal(RQ[:], SMB[:]),
                         reads=["smb"], writes=["rq"])
                units.append(u_smalls)

                def u_norm(m):
                    for t_i in range(2):
                        col = 2 * m + t_i
                        P.op("v", lambda v, m=m, t_i=t_i, col=col:
                             v.tensor_scalar(
                                 QN[:, t_i * N + m * 128:
                                    t_i * N + m * 128 + 128],
                                 QR[:, m * 256 + t_i * 128:
                                    m * 256 + t_i * 128 + 128],
                                 MEANS[:, col:col + 1], RQ[:, col:col + 1],
                                 AOT.subtract, AOT.mult),
                             reads=[f"qr/{m}/{t_i}", "means", "rq"],
                             writes=[f"qn/{m}/{t_i}"])
                for m in range(MC):
                    units.append(lambda m=m: u_norm(m))

                def u_final():
                    qnkeys = [f"qn/{m}/{t}" for m in range(MC)
                              for t in range(2)]
                    for t_i, dst, nm in ((0, QFM, "qfm"), (1, KFM, "kfm")):
                        P.dma("y", lambda y, t_i=t_i, dst=dst, par=par:
                              y.dma_start_transpose(
                                  AP(dst[par], 0,
                                     [[N, 128], [128, 16], [1, 128]]),
                                  QN[:, t_i * N:(t_i + 1) * N]),
                              reads=qnkeys,
                              writes=[f"{nm}{par}/{m}" for m in range(MC)],
                              slot=f"qnT{t_i}")
                    qk = [f"qfm{par}/{m}" for m in range(MC)]
                    kk = [f"kfm{par}/{m}" for m in range(MC)]
                    P.op("s", lambda s, par=par: s.activation(
                        QFM[par][:], QFM[par][:], ACTF.Identity, bias=BET[:],
                        scale=GAM[:]),
                        reads=qk + ["GAM", "BET"], writes=[f"qfmF{par}"] + qk)
                    P.op("s", lambda s, par=par: s.activation(
                        KFM[par][:], KFM[par][:], ACTF.Identity, bias=BET[:],
                        scale=GAM[:]),
                        reads=kk + ["GAM", "BET"], writes=[f"kfmF{par}"] + kk)
                units.append(u_final)
                return units

            def attn_units(hh):
                par = hh % 2
                vkeys = [f"vpad{par}/{m}" for m in range(MC)] + [f"vpadones{par}"]
                units = []

                def u_s(i, j):
                    r = j - 4 * i
                    s0 = 128 * max(0, r)
                    nj = 512 - s0
                    jb = j % 6
                    pb_ = j % 2
                    psk = f"ps{pb_}"
                    P.op("t", lambda t, i=i, j=j, s0=s0, nj=nj, pb_=pb_,
                         par=par: t.matmul(
                             PS[pb_][:, 0:nj],
                             KFM[par][:, j * 128:(j + 1) * 128],
                             QFM[par][:, i * 512 + s0:(i + 1) * 512],
                             start=True, stop=True),
                         reads=[f"qfmF{par}", f"kfmF{par}"], writes=[psk])
                    j2 = j % 4
                    P.op("s", lambda s, j2=j2, pb_=pb_, nj=nj: s.activation(
                        S2B[j2][:, 0:nj], PS[pb_][:, 0:nj], ACTF.Square),
                        reads=[psk], writes=[f"s2{j2}"])
                    P.op("s", lambda s, jb=jb, j2=j2, nj=nj: s.activation(
                        S4B[jb][:, 0:nj], S2B[j2][:, 0:nj], ACTF.Square),
                        reads=[f"s2{j2}"], writes=[f"s4{jb}"])
                    if r >= 0:
                        P.op("v", lambda v, jb=jb: v.tensor_tensor(
                            S4B[jb][:, 0:128], S4B[jb][:, 0:128], TRI[:],
                            AOT.mult),
                            reads=[f"s4{jb}", "TRI"], writes=[f"s4{jb}"])

                def u_pv(i, j):
                    r = j - 4 * i
                    s0 = 128 * max(0, r)
                    jb = j % 6
                    for sub in range(max(0, r), 4):
                        P.op("t", lambda t, jb=jb, sub=sub, s0=s0, j=j,
                             par=par, i=i: t.matmul(
                                 PV[sub][:, 0:129],
                                 S4B[jb][:, 128 * sub - s0:
                                         128 * sub - s0 + 128],
                                 VPAD[par][:, j * 129: j * 129 + 129],
                                 start=(j == 0), stop=(j == 4 * i + sub)),
                             reads=[f"s4{jb}"] + vkeys, writes=[f"pv{sub}"])

                def u_epilogue(i):
                    for sub in range(4):
                        P.op("v", lambda v, sub=sub, i=i, par=par: v.tensor_copy(
                            DEN[par][:, i * 4 + sub: i * 4 + sub + 1],
                            PV[sub][:, 128:129]),
                            reads=[f"pv{sub}"], writes=[f"den{par}/{i}/{sub}"])
                    dk = [f"den{par}/{i}/{s_}" for s_ in range(4)]
                    P.op("s", lambda s, i=i, par=par: s.activation(
                        RINV[par][:, i * 4: i * 4 + 4],
                        DEN[par][:, i * 4: i * 4 + 4], ACTF.Ln),
                        reads=dk, writes=[f"lnden{par}/{i}"])
                    P.op("s", lambda s, i=i, par=par: s.activation(
                        RINV[par][:, i * 4: i * 4 + 4],
                        RINV[par][:, i * 4: i * 4 + 4], ACTF.Exp, scale=-1.0),
                        reads=[f"lnden{par}/{i}"], writes=[f"rinvA{par}/{i}"])
                    for sub in range(4):
                        blk = i * 4 + sub
                        P.op("v", lambda v, sub=sub, blk=blk, par=par, i=i:
                             v.tensor_scalar(
                                 OUTH[par][:, blk * 128:(blk + 1) * 128],
                                 PV[sub][:, 0:128],
                                 RINV[par][:, blk:blk + 1], None,
                                 AOT.mult, AOT.bypass),
                             reads=[f"pv{sub}", f"rinvA{par}/{i}"],
                             writes=[f"outh{par}/{blk}"])

                def u_spill():
                    P.dma("y", lambda y, par=par: y.dma_start_transpose(
                        AP(AFM[par], 0, [[N, 128], [128, 16], [1, 128]]),
                        AP(OUTH[par], 0, [[N, 128], [128, 16], [1, 128]])),
                        reads=[f"outh{par}/{b_}" for b_ in range(16)],
                        writes=[f"afm{par}/{b_}" for b_ in range(16)],
                        slot=f"afmT{par}")
                    akeys = [f"afm{par}/{b_}" for b_ in range(16)]
                    P.dma("g", lambda g, hh=hh, par=par: g.dma_start(
                        out=attnt_h[hh * 128:(hh + 1) * 128, :],
                        in_=AFM[par][:]),
                        reads=akeys, writes=[f"attnsp/{hh}"],
                        slot=f"attnsp{par}")

                LAG = 4
                for i in range(NB):
                    nj_ = 4 * i + 4
                    for j in range(nj_):
                        def u_sj(i=i, j=j):
                            u_s(i, j)
                            if j >= LAG:
                                u_pv(i, j - LAG)
                        units.append(u_sj)

                    def u_end(i=i, nj_=nj_):
                        for j in range(max(0, nj_ - LAG), nj_):
                            u_pv(i, j)
                        u_epilogue(i)
                    units.append(u_end)
                units.append(u_spill)
                return units

            XB0B = XB[0].bitcast(BF16)
            XB1B = XB[1].bitcast(BF16)
            P.op("v", lambda v: v.memset(ALPHACC[:], 0.0), writes=["alacc"])

            def wc_plane(kc):
                return XB0B[:, 0:E] if kc % 2 == 0 else XB1B[:, E:2 * E]

            def swt_load(kc):
                wp = kc % 2
                P.dma("g", lambda g, kc=kc: g.dma_start(
                    out=wc_plane(kc),
                    in_=AP(fw, kc * 128, [[E, 128], [E * 128, 16], [1, 128]])),
                    writes=[f"wcA{wp}", "xb0" if wp == 0 else "xb1"],
                    slot=f"wcA{wp}")

            def swt_unit(kc):
                wp = kc % 2
                wpl = wc_plane(kc)
                P.op("v", lambda v, wpl=wpl: v.tensor_scalar(
                    XB1B[:, 0:E], wpl, cp(C_ZERO), None, AOT.is_ge,
                    AOT.bypass),
                    reads=[f"wcA{wp}", "CP0"], writes=["sgA", "xb1"])
                P.op("v", lambda v: v.tensor_scalar(
                    XB1B[:, 0:E], XB1B[:, 0:E], cp(C_TWO), cp(C_ONE), AOT.mult,
                    AOT.subtract),
                    reads=["sgA", "CP4", "CP5"], writes=["sgA"])
                P.op("v", lambda v, wpl=wpl: v.tensor_reduce(
                    ALPHA[:], AP(wpl.tensor, wpl.offset,
                                 [[2 * E, 128], [128, 16], [1, 128]]),
                    AX.X, AOT.add, apply_absolute_value=True),
                    reads=[f"wcA{wp}"], writes=["alpart"])
                P.op("v", lambda v: v.tensor_tensor(
                    ALPHACC[:], ALPHACC[:], ALPHA[:], AOT.add),
                    reads=["alpart", "alacc"], writes=["alacc"])
                P.dma("y", lambda y: y.dma_start_transpose(
                    AP(XB0B, E, [[2 * E, 128], [128, 16], [1, 128]]),
                    XB1B[:, 0:E]),
                    reads=["sgA"], writes=["stgA", "xb0"], slot="swtT")
                P.dma("g", lambda g, kc=kc: g.dma_start(
                    out=swt_h[kc * 128:(kc + 1) * 128, :], in_=XB0B[:, E:2 * E]),
                    reads=["stgA"], writes=[f"swtsp/{kc}"], slot="swtsp")

            lnxt_keys = [f"lnxT/{kc}/{c}" for kc in range(KC)
                         for c in range(MC)]
            emit_w(0)
            emit_w(1)

            uq0 = qkv_units(0)
            for c in range(MC):
                b = c % 2
                xk = f"xb{b}"
                P.dma("g", lambda g, c=c, b=b: g.dma_start(
                    out=XB[b][:], in_=xh[c * 128:(c + 1) * 128, :]),
                    writes=[xk], slot=xk)
                P.op("v", lambda v, b=b: v.tensor_reduce(
                    XS[:, 0:1], XB[b][:], AX.X, AOT.add),
                    reads=[xk], writes=["xsum"])
                P.op("s", lambda s, b=b: s.activation(
                    SCR[:], XB[b][:], ACTF.Square, accum_out=XS[:, 1:2]),
                    reads=[xk], writes=["xss", "scrA"])
                ln_smalls(XS, "x", "xsum", "xss")
                P.op("v", lambda v: v.tensor_scalar(
                    XS[:, 7:8], XS[:, 0:1], cp(C_NI2048), None,
                    AOT.mult, AOT.bypass),
                    reads=["xsum", "CP6"], writes=["xnm"])
                P.op("v", lambda v: v.tensor_tensor(
                    XS[:, 7:8], XS[:, 7:8], XS[:, 6:7], AOT.mult),
                    reads=["xnm", "xr"], writes=["xnm"])
                P.op("s", lambda s, b=b: s.activation(
                    LNC[b][:], XB[b][:], ACTF.Identity, bias=XS[:, 7:8],
                    scale=XS[:, 6:7]),
                    reads=[xk, "xr", "xnm"], writes=[f"lnc{b}"])
                P.dma("y", lambda y, c=c, b=b: y.dma_start_transpose(
                    AP(LNXT, c * 128, [[KC * N, 128], [N, 16], [1, 128]]),
                    AP(LNC[b], 0, [[E, 128], [128, 16], [1, 128]])),
                    reads=[f"lnc{b}"],
                    writes=[f"lnxT/{kc}/{c}" for kc in range(KC)],
                    slot=f"lnxTd{b}")
                if c >= 2:
                    uq0[c - 2]()
            for th in uq0[MC - 2:]:
                th()
            emit_w(2)
            swt_load(0)

            for h in range(1, H + 1):
                uq = qkv_units(h) if h < H else []
                ua = attn_units(h - 1) if h >= 1 else []
                ia = 0
                SHIFT = 3
                for iq, th in enumerate(uq):
                    th()
                    want = int(round((iq + 1 - SHIFT) * len(ua)
                                     / max(1, len(uq) - SHIFT)))
                    while ia < min(max(0, want), len(ua)):
                        ua[ia]()
                        ia += 1
                while ia < len(ua):
                    ua[ia]()
                    ia += 1
                if h + 2 < H:
                    emit_w(h + 2)
                if h == 0:
                    swt_load(0)
                elif h < H:
                    swt_unit(h - 1)
                    swt_load(h)
                else:
                    swt_unit(H - 1)
                if h == H - 1:
                    P.dma("g", lambda g: g.dma_start(
                        out=AP(LNXT, 0, [[KC * N, 128], [N, 16], [1, N]]),
                        in_=AP(wout, 0, [[E, 128], [E * 128, 16], [1, E]])),
                        writes=lnxt_keys + ["woutP"], slot="woutP")
            P.op("v", lambda v: v.tensor_scalar(
                ALPHA[:], ALPHACC[:], cp(C_I2048), None, AOT.mult, AOT.bypass),
                reads=["alacc", "CP2"], writes=["alphaF", "alpart"])

        P.fence()

        with contextlib.ExitStack() as sB:
            TB = sB.enter_context
            ATK = [TB(nc.sbuf_tensor(f"ATK{i}", [128, KC * 512], BF16))
                   for i in range(2)]
            XB2 = [TB(nc.sbuf_tensor(f"XB2{i}", [128, E], F32)) for i in range(4)]
            APJT = [TB(nc.sbuf_tensor(f"APJT{i}", [128, 512], BF16))
                    for i in range(4)]
            APJ = [TB(nc.sbuf_tensor(f"APJ{i}", [128, 4 * E], BF16))
                   for i in range(2)]
            SCB = TB(nc.sbuf_tensor("SCB", [128, E], BF16))
            O1 = [TB(nc.sbuf_tensor(f"O1{i}", [128, E], F32)) for i in range(2)]
            LN2C = [TB(nc.sbuf_tensor(f"LN2C{i}", [128, E], BF16))
                    for i in range(2)]
            L2F = [TB(nc.sbuf_tensor(f"L2F{i}", [128, KC * 128], BF16))
                   for i in range(2)]
            BS = TB(nc.sbuf_tensor("BS", [128, 32], F32))
            PB = [TB(nc.psum_tensor(f"PB{i}", [128, 512], F32)) for i in range(4)]

            P.barrier("g", ["g:attnsp0", "g:attnsp1"], writes=["attn_done"])

            def load_atk(tb):
                tbb = tb % 2
                P.dma("g", lambda g, tb=tb, tbb=tbb: g.dma_start(
                    out=ATK[tbb][:],
                    in_=AP(attnt_h, tb * 512,
                           [[N, 128], [N * 128, 16], [1, 512]])),
                    reads=["attn_done"], writes=[f"atk{tbb}"], slot=f"atk{tbb}")

            def b_s_unit(tb, s_):
                tbb = tb % 2
                pb = s_ % 4
                for kc in range(KC):
                    P.op("t", lambda t, s_=s_, kc=kc, pb=pb, tbb=tbb: t.matmul(
                        PB[pb][:],
                        LNXT[:, kc * N + s_ * 128: kc * N + s_ * 128 + 128],
                        ATK[tbb][:, kc * 512:(kc + 1) * 512],
                        start=(kc == 0), stop=(kc == KC - 1)),
                        reads=["woutP", f"atk{tbb}"], writes=[f"pb{pb}"])
                P.op("s", lambda s, pb=pb: s.activation(
                    APJT[pb][:], PB[pb][:], ACTF.Identity),
                    reads=[f"pb{pb}"], writes=[f"apjT{pb}"])
                P.dma("y", lambda y, s_=s_, pb=pb, tbb=tbb:
                      y.dma_start_transpose(
                          AP(APJ[tbb], s_ * 128,
                             [[4 * E, 128], [E, 4], [1, 128]]),
                          AP(APJT[pb], 0, [[512, 128], [128, 4], [1, 128]])),
                      reads=[f"apjT{pb}"],
                      writes=[f"apj{tbb}/{cl}/{s_}" for cl in range(4)],
                      slot=f"apjTd{pb}")

            def b_cl_unit(tb, cl):
                tbb = tb % 2
                c = tb * 4 + cl
                cb = c % 2
                b0 = cl * 8
                kp = f"B{cl}"
                apk = [f"apj{tbb}/{cl}/{s_}" for s_ in range(16)]
                P.op("v", lambda v, cl=cl, tbb=tbb, b0=b0: v.tensor_reduce(
                    BS[:, b0:b0 + 1], APJ[tbb][:, cl * E:(cl + 1) * E], AX.X,
                    AOT.add), reads=apk, writes=[kp + "sum"])
                P.op("s", lambda s, cl=cl, tbb=tbb, b0=b0: s.activation(
                    SCB[:], APJ[tbb][:, cl * E:(cl + 1) * E], ACTF.Square,
                    accum_out=BS[:, b0 + 1:b0 + 2]),
                    reads=apk, writes=[kp + "ss", "scb"])
                ln_smalls(BS, kp, kp + "sum", kp + "ss", b0=b0)
                P.op("v", lambda v, cl=cl, tbb=tbb, cb=cb, b0=b0:
                     v.tensor_scalar(
                         O1[cb][:], APJ[tbb][:, cl * E:(cl + 1) * E],
                         BS[:, b0 + 2:b0 + 3], BS[:, b0 + 6:b0 + 7],
                         AOT.subtract, AOT.mult),
                     reads=apk + [kp + "m", kp + "r"], writes=[f"o1{cb}"])
                P.op("v", lambda v, cb=cb, c=c, b0=b0: v.scalar_tensor_tensor(
                    O1[cb][:], O1[cb][:], cp(C_ZERO), XB2[c % 4][:], AOT.add,
                    AOT.add, accum_out=BS[:, b0:b0 + 1]),
                    reads=[f"o1{cb}", f"xb2{c % 4}", "CP0"],
                    writes=[f"o1{cb}", kp + "sum"])
                P.dma("g", lambda g, c=c, cb=cb: g.dma_start(
                    out=out1_h[c * 128:(c + 1) * 128, :], in_=O1[cb][:]),
                    reads=[f"o1{cb}"], writes=[f"o1sp/{c}"], slot=f"o1sp{cb}")
                P.op("s", lambda s, cb=cb, b0=b0: s.activation(
                    SCB[:], O1[cb][:], ACTF.Square,
                    accum_out=BS[:, b0 + 1:b0 + 2]),
                    reads=[f"o1{cb}"], writes=[kp + "ss", "scb"])
                ln_smalls(BS, kp + "L2", kp + "sum", kp + "ss", b0=b0)
                P.op("v", lambda v, cb=cb, b0=b0: v.tensor_scalar(
                    LN2C[cb][:], O1[cb][:], BS[:, b0 + 2:b0 + 3],
                    BS[:, b0 + 6:b0 + 7], AOT.subtract, AOT.mult),
                    reads=[f"o1{cb}", kp + "L2m", kp + "L2r"],
                    writes=[f"ln2c{cb}"])
                P.dma("y", lambda y, cb=cb: y.dma_start_transpose(
                    AP(L2F[cb], 0, [[KC * 128, 128], [128, 16], [1, 128]]),
                    AP(LN2C[cb], 0, [[E, 128], [128, 16], [1, 128]])),
                    reads=[f"ln2c{cb}"],
                    writes=[f"l2f{cb}/{kc}" for kc in range(KC)],
                    slot=f"l2fT{cb}")
                P.dma("g", lambda g, c=c, cb=cb: g.dma_start(
                    out=AP(lnx2t_h, c * 128,
                           [[N, 128], [N * 128, 16], [1, 128]]),
                    in_=L2F[cb][:]),
                    reads=[f"l2f{cb}/{kc}" for kc in range(KC)],
                    writes=[f"l2sp/{c}"], slot=f"l2sp{cb}")

            def load_xb2(c):
                P.dma("g", lambda g, c=c: g.dma_start(
                    out=XB2[c % 4][:], in_=xh[c * 128:(c + 1) * 128, :]),
                    writes=[f"xb2{c % 4}"], slot=f"xb2{c % 4}")

            load_atk(0)
            for cl in range(4):
                load_xb2(cl)
            for tb in range(4):
                if tb + 1 < 4:
                    load_atk(tb + 1)
                for s_ in range(16):
                    b_s_unit(tb, s_)
                    if tb >= 1 and s_ % 4 == 3:
                        cl = s_ // 4
                        b_cl_unit(tb - 1, cl)
                        load_xb2(tb * 4 + cl)
            for cl in range(4):
                b_cl_unit(3, cl)

            P.dma("g", lambda g: g.dma_start(
                out=AP(LNXT, 0, [[KC * N, 128], [N, 16], [1, N]]),
                in_=AP(swt_h, 0, [[N, 128], [N * 128, 16], [1, N]])),
                reads=[f"swtsp/{kc}" for kc in range(KC)],
                writes=["woutP", "swtP"], slot="swtP")

        P.fence()

        with contextlib.ExitStack() as sC:
            TC = sC.enter_context
            LX = [TC(nc.sbuf_tensor(f"LX{i}", [128, KC * 512], BF16))
                  for i in range(2)]
            YTT = [TC(nc.sbuf_tensor(f"YTT{i}", [128, 512], BF16))
                   for i in range(4)]
            YTM = [TC(nc.sbuf_tensor(f"YTM{i}", [128, 4 * E], BF16))
                   for i in range(2)]
            O1R = [TC(nc.sbuf_tensor(f"O1R{i}", [128, E], F32)) for i in range(4)]
            OC = [TC(nc.sbuf_tensor(f"OC{i}", [128, E], F32)) for i in range(2)]
            SCC = TC(nc.sbuf_tensor("SCC", [128, E], BF16))
            CS = TC(nc.sbuf_tensor("CS", [128, 32], F32))
            PC = [TC(nc.psum_tensor(f"PC{i}", [128, 512], F32)) for i in range(4)]

            P.barrier("g", ["g:l2sp0", "g:l2sp1"], writes=["l2_done"])
            P.barrier("g", ["g:o1sp0", "g:o1sp1"], writes=["o1_done"])

            def load_lx(tb):
                tbb = tb % 2
                P.dma("g", lambda g, tb=tb, tbb=tbb: g.dma_start(
                    out=LX[tbb][:],
                    in_=AP(lnx2t_h, tb * 512,
                           [[N, 128], [N * 128, 16], [1, 512]])),
                    reads=["l2_done"], writes=[f"lx{tbb}"], slot=f"lx{tbb}")

            def c_s_unit(tb, s_):
                tbb = tb % 2
                pb = s_ % 4
                for kc in range(KC):
                    P.op("t", lambda t, s_=s_, kc=kc, pb=pb, tbb=tbb: t.matmul(
                        PC[pb][:],
                        LNXT[:, kc * N + s_ * 128: kc * N + s_ * 128 + 128],
                        LX[tbb][:, kc * 512:(kc + 1) * 512],
                        start=(kc == 0), stop=(kc == KC - 1)),
                        reads=["swtP", f"lx{tbb}"], writes=[f"pc{pb}"])
                P.op("s", lambda s, s_=s_, pb=pb: s.activation(
                    YTT[pb][:], PC[pb][:], ACTF.Identity,
                    bias=BIASF[:, s_:s_ + 1], scale=ALPHA[:, s_:s_ + 1]),
                    reads=[f"pc{pb}", "alphaF", "BIASF"], writes=[f"ytt{pb}"])
                P.dma("y", lambda y, s_=s_, pb=pb, tbb=tbb:
                      y.dma_start_transpose(
                          AP(YTM[tbb], s_ * 128,
                             [[4 * E, 128], [E, 4], [1, 128]]),
                          AP(YTT[pb], 0, [[512, 128], [128, 4], [1, 128]])),
                      reads=[f"ytt{pb}"],
                      writes=[f"ytm{tbb}/{cl}/{s_}" for cl in range(4)],
                      slot=f"ytmTd{pb}")

            def c_cl_unit(tb, cl):
                tbb = tb % 2
                c = tb * 4 + cl
                cb = c % 2
                b0 = cl * 8
                kp = f"C{cl}"
                ymk = [f"ytm{tbb}/{cl}/{s_}" for s_ in range(16)]
                P.op("v", lambda v, cl=cl, tbb=tbb, b0=b0: v.tensor_reduce(
                    CS[:, b0:b0 + 1], YTM[tbb][:, cl * E:(cl + 1) * E], AX.X,
                    AOT.add), reads=ymk, writes=[kp + "sum"])
                P.op("s", lambda s, cl=cl, tbb=tbb, b0=b0: s.activation(
                    SCC[:], YTM[tbb][:, cl * E:(cl + 1) * E], ACTF.Square,
                    accum_out=CS[:, b0 + 1:b0 + 2]),
                    reads=ymk, writes=[kp + "ss", "sccS"])
                ln_smalls(CS, kp, kp + "sum", kp + "ss", b0=b0)
                P.op("v", lambda v, cl=cl, tbb=tbb, cb=cb, b0=b0:
                     v.tensor_scalar(
                         OC[cb][:], YTM[tbb][:, cl * E:(cl + 1) * E],
                         CS[:, b0 + 2:b0 + 3], CS[:, b0 + 6:b0 + 7],
                         AOT.subtract, AOT.mult),
                     reads=ymk + [kp + "m", kp + "r"], writes=[f"oc{cb}"])
                P.op("v", lambda v, cb=cb, c=c: v.tensor_tensor(
                    OC[cb][:], OC[cb][:], O1R[c % 4][:], AOT.add),
                    reads=[f"oc{cb}", f"o1r{c % 4}"], writes=[f"oc{cb}"])
                P.dma("g", lambda g, c=c, cb=cb: g.dma_start(
                    out=out_h[c * 128:(c + 1) * 128, :], in_=OC[cb][:]),
                    reads=[f"oc{cb}"], writes=[f"outsp/{c}"],
                    slot=f"outsp{cb}")

            def load_o1r(c):
                P.dma("g", lambda g, c=c: g.dma_start(
                    out=O1R[c % 4][:], in_=out1_h[c * 128:(c + 1) * 128, :]),
                    reads=["o1_done"], writes=[f"o1r{c % 4}"],
                    slot=f"o1r{c % 4}")

            load_lx(0)
            for cl in range(4):
                load_o1r(cl)
            for tb in range(4):
                if tb + 1 < 4:
                    load_lx(tb + 1)
                for s_ in range(16):
                    c_s_unit(tb, s_)
                    if tb >= 1 and s_ % 4 == 3:
                        cl = s_ // 4
                        c_cl_unit(tb - 1, cl)
                        load_o1r(tb * 4 + cl)
            for cl in range(4):
                c_cl_unit(3, cl)

            P.barrier("g", ["g:outsp0", "g:outsp1"], writes=["all_done"])

            P.finalize_marks()
            with contextlib.ExitStack() as se:
                SE = se.enter_context
                sems = {e: SE(nc.semaphore(f"sem_{e}")) for e in ("t", "v", "s")}
                slot_sems = {}
                for i_, sk in enumerate(P.slots):
                    slot_sems[sk] = SE(nc.semaphore(f"sd{i_}"))
                block = SE(nc.Block())

                @block.tensor
                def _(t):
                    P.emit_one("t", t, sems, slot_sems)

                @block.vector
                def _(v):
                    P.emit_one("v", v, sems, slot_sems)

                @block.scalar
                def _(s):
                    P.emit_one("s", s, sems, slot_sems)

                @block.gpsimd
                def _(g):
                    P.emit_one("g", g, sems, slot_sems)

                @block.sync
                def _(y):
                    P.emit_one("y", y, sems, slot_sems)

    return nc


_CACHE = {}


def kernel(L0_attn_x, L0_attn_w_qkv, L0_attn_w_out, L0_attn_ln_scale,
           L0_attn_ln_bias, L0_ffn_weight, L0_ffn_bias):
    x = np.ascontiguousarray(np.asarray(L0_attn_x, dtype=np.float32))
    consts = host_consts()
    if "nc" not in _CACHE:
        _CACHE["nc"] = build_nc()
    nc = _CACHE["nc"]
    base = {
        "wqkv": np.ascontiguousarray(np.asarray(L0_attn_w_qkv, np.float32)),
        "wout": np.ascontiguousarray(np.asarray(L0_attn_w_out, np.float32)),
        "gam": np.ascontiguousarray(np.asarray(L0_attn_ln_scale, np.float32)),
        "bet": np.ascontiguousarray(np.asarray(L0_attn_ln_bias, np.float32)),
        "fw": np.ascontiguousarray(np.asarray(L0_ffn_weight, np.float32)),
        "fb": np.ascontiguousarray(np.asarray(L0_ffn_bias, np.float32)),
        "ctab": consts["ctab"], "stab": consts["stab"], "tri": consts["tri"],
    }
    in_maps = [dict(base, xb=np.ascontiguousarray(x[b])) for b in range(8)]
    res = run_bass_kernel_spmd(nc, in_maps, core_ids=list(range(8)))
    return np.stack([np.asarray(res.results[b]["out"], np.float32)
                     for b in range(8)], axis=0)


# version 19
# speedup vs baseline: 1.2576x; 1.0753x over previous
import contextlib
import numpy as np
import ml_dtypes
import concourse.bass as bass
import concourse.mybir as mybir
from concourse.ap import AP
from concourse.bass_utils import run_bass_kernel_spmd

F32 = mybir.dt.float32
BF16 = mybir.dt.bfloat16
AOT = mybir.AluOpType
ACTF = mybir.ActivationFunctionType
AX = mybir.AxisListType

N = 2048
E = 2048
H = 16
D = 128
MC = 16
KC = 16
NB = 4
EPS = 1e-5


class Op:
    __slots__ = ("eng", "fn", "deps", "inc", "slot", "use", "idx", "cum",
                 "dma_waits", "drain_upto")

    def __init__(self, eng, fn, idx):
        self.eng = eng
        self.fn = fn
        self.deps = []
        self.inc = False
        self.slot = None
        self.use = 0
        self.idx = idx
        self.cum = 0
        self.dma_waits = ()
        self.drain_upto = -1


class Prog:
    ENGS = ("t", "v", "s", "g", "y")

    def __init__(self):
        self.streams = {e: [] for e in self.ENGS}
        self.writer = {}
        self.readers = {}
        self.slots = {}

    def _dep(self, op, dep):
        if dep is op or dep is None:
            return
        if dep.slot is None and dep.eng == op.eng and (
                dep.eng in ("t", "g", "y") or dep.fn is None):
            return
        if dep.slot is not None:
            val = 16 * self.slots[dep.slot]
            for i, (d, v) in enumerate(op.deps):
                if d is dep:
                    if val > v:
                        op.deps[i] = (dep, val)
                    return
            op.deps.append((dep, val))
            return
        for d, _ in op.deps:
            if d is dep:
                return
        op.deps.append((dep, None))

    def _record(self, eng, fn, reads, writes, slot, dma_waits):
        op = Op(eng, fn, len(self.streams[eng]))
        if slot is not None:
            op.slot = slot
        op.dma_waits = tuple(dma_waits)
        for k in reads:
            w = self.writer.get(k)
            if w is not None:
                self._dep(op, w)
        for k in writes:
            rd = self.readers.get(k)
            if rd:
                for r in rd.values():
                    self._dep(op, r)
            w = self.writer.get(k)
            if w is not None:
                self._dep(op, w)
        if slot is not None:
            self.slots[slot] = self.slots.get(slot, 0) + 1
            op.use = self.slots[slot]
        for k in reads:
            self.readers.setdefault(k, {})[eng] = op
        for k in writes:
            self.writer[k] = op
            self.readers[k] = {}
        self.streams[eng].append(op)
        return op

    def op(self, eng, fn, reads=(), writes=()):
        assert eng in ("t", "v", "s")
        return self._record(eng, fn, reads, writes, None, ())

    def dma(self, ring, fn, reads=(), writes=(), slot=None):
        assert ring in ("g", "y")
        if slot is None:
            slot = "dma:" + (writes[0] if writes else "anon")
        return self._record(ring, fn, reads, writes, ring + ":" + slot, ())

    def barrier(self, eng, slot_waits, writes=()):
        waits = [(k, 16 * self.slots.get(k, 0)) for k in slot_waits]
        return self._record(eng, None, (), writes, None, waits)

    def fence(self):
        lasts = {e: (self.streams[e][-1] if self.streams[e] else None)
                 for e in self.ENGS}
        slot_totals = [(k, 16 * v) for k, v in self.slots.items()]
        for e in self.ENGS:
            op = self._record(e, None, (), (), None, slot_totals)
            for e2, lo in lasts.items():
                if lo is not None and e2 != e and lo.slot is None:
                    self._dep(op, lo)

    def finalize_marks(self):
        for e in self.ENGS:
            for op in self.streams[e]:
                for dep, _ in op.deps:
                    dep.inc = True
        for e in ("t", "v", "s"):
            c = 0
            for op in self.streams[e]:
                if op.inc:
                    c += 1
                op.cum = c

    def emit_one(self, e, eng, sems, slot_sems):
        waited = {}
        drained = -1
        for i, op in enumerate(self.streams[e]):
            for sk, val in op.dma_waits:
                hs = slot_sems[sk]
                if waited.get(id(hs), -1) < val:
                    eng.wait_ge(hs, val)
                    waited[id(hs)] = val
            for dep, dval in op.deps:
                if dep.slot is not None:
                    hs = slot_sems[dep.slot]
                    val = dval
                else:
                    hs = sems[dep.eng]
                    val = dep.cum
                if waited.get(id(hs), -1) < val:
                    eng.wait_ge(hs, val)
                    waited[id(hs)] = val
            if op.drain_upto > drained:
                eng.drain()
                drained = i - 1
            if op.fn is None:
                if op.inc and e in sems:
                    eng.nop().then_inc(sems[e], 1)
                continue
            inst = op.fn(eng)
            if op.slot is not None:
                inst.then_inc(slot_sems[op.slot], 16)
            elif op.inc:
                inst.then_inc(sems[e], 1)


def host_consts():
    inv = 1.0 / (10000.0 ** (np.arange(0, D, 2, dtype=np.float32) / D))
    t = np.arange(N, dtype=np.float32)[:, None] * inv[None, :]
    cos = np.cos(t).astype(np.float32)
    sin = np.sin(t).astype(np.float32)
    return {
        "ctab": np.ascontiguousarray(
            np.concatenate([cos, cos], 1).astype(ml_dtypes.bfloat16)),
        "stab": np.ascontiguousarray(
            np.concatenate([-sin, sin], 1).astype(ml_dtypes.bfloat16)),
        "tri": np.triu(np.ones((128, 128), np.float32)).astype(ml_dtypes.bfloat16),
    }


def build_nc():
    nc = bass.Bass("TRN2", target_bir_lowering=False)

    xh = nc.dram_tensor("xb", [N, E], F32, kind="ExternalInput")
    wqkv = nc.dram_tensor("wqkv", [E, 3 * E], F32, kind="ExternalInput")
    wout = nc.dram_tensor("wout", [E, E], F32, kind="ExternalInput")
    gam = nc.dram_tensor("gam", [D], F32, kind="ExternalInput")
    bet = nc.dram_tensor("bet", [D], F32, kind="ExternalInput")
    fw = nc.dram_tensor("fw", [E, E], F32, kind="ExternalInput")
    fb = nc.dram_tensor("fb", [E], F32, kind="ExternalInput")
    ctabh = nc.dram_tensor("ctab", [N, D], BF16, kind="ExternalInput")
    stabh = nc.dram_tensor("stab", [N, D], BF16, kind="ExternalInput")
    trih = nc.dram_tensor("tri", [128, 128], BF16, kind="ExternalInput")
    out_h = nc.dram_tensor("out", [N, E], F32, kind="ExternalOutput")
    attnt_h = nc.dram_tensor("attnt_sc", [E, N], BF16, kind="Internal")
    out1_h = nc.dram_tensor("out1_sc", [N, E], F32, kind="Internal")
    lnx2t_h = nc.dram_tensor("lnx2t_sc", [E, N], BF16, kind="Internal")
    swt_h = nc.dram_tensor("swt_sc", [E, N], BF16, kind="Internal")

    P = Prog()
    with contextlib.ExitStack() as top:
        TE = top.enter_context
        LNXT = TE(nc.sbuf_tensor("LNXT", [128, KC * N], BF16))
        GAM = TE(nc.sbuf_tensor("GAM", [128, 1], F32))
        BET = TE(nc.sbuf_tensor("BET", [128, 1], F32))
        BIASF = TE(nc.sbuf_tensor("BIASF", [128, 16], F32))
        CP = TE(nc.sbuf_tensor("CP", [128, 8], F32))
        ALPHA = TE(nc.sbuf_tensor("ALPHA", [128, 16], F32))
        ALPHACC = TE(nc.sbuf_tensor("ALPHACC", [128, 16], F32))

        C_ZERO, C_I128, C_I2048, C_EPS, C_TWO, C_ONE, C_NI2048 = \
            0, 1, 2, 3, 4, 5, 6

        def cp(col):
            return CP[:, col:col + 1]

        def ld(dst_ap, src_ap, key, slow=False):
            P.dma("g", lambda g, d=dst_ap, s=src_ap, sl=slow: g.dma_start(
                out=d, in_=s, allow_slow_non_contiguous=sl),
                writes=[key], slot=key)

        ld(GAM[:], AP(gam, 0, [[1, 128], [1, 1]]), "GAM", slow=True)
        ld(BET[:], AP(bet, 0, [[1, 128], [1, 1]]), "BET", slow=True)
        ld(BIASF[:], AP(fb, 0, [[1, 128], [128, 16]]), "BIASF", slow=True)
        for col, val in [(C_ZERO, 0.0), (C_I128, 1.0 / 128.0),
                         (C_I2048, 1.0 / 2048.0), (C_EPS, EPS),
                         (C_TWO, 2.0), (C_ONE, 1.0),
                         (C_NI2048, -1.0 / 2048.0)]:
            P.op("v", lambda v, c=col, vv=val: v.memset(CP[:, c:c + 1], vv),
                 writes=[f"CP{col}"])

        def ln_smalls(S, kp, sumk, ssk, b0=0):
            P.op("v", lambda v: v.tensor_scalar(
                S[:, b0 + 2:b0 + 3], S[:, b0:b0 + 1], cp(C_I2048), None,
                AOT.mult, AOT.bypass), reads=[sumk, "CP2"], writes=[kp + "m"])
            P.op("v", lambda v: v.tensor_scalar(
                S[:, b0 + 3:b0 + 4], S[:, b0 + 1:b0 + 2], cp(C_I2048), None,
                AOT.mult, AOT.bypass), reads=[ssk], writes=[kp + "e"])
            P.op("v", lambda v: v.tensor_tensor(
                S[:, b0 + 4:b0 + 5], S[:, b0 + 2:b0 + 3], S[:, b0 + 2:b0 + 3],
                AOT.mult), reads=[kp + "m"], writes=[kp + "q"])
            P.op("v", lambda v: v.tensor_tensor(
                S[:, b0 + 4:b0 + 5], S[:, b0 + 3:b0 + 4], S[:, b0 + 4:b0 + 5],
                AOT.subtract), reads=[kp + "e", kp + "q"], writes=[kp + "q"])
            P.op("s", lambda s: s.activation(
                S[:, b0 + 5:b0 + 6], S[:, b0 + 4:b0 + 5], ACTF.Sqrt,
                bias=cp(C_EPS)), reads=[kp + "q", "CP3"], writes=[kp + "sd"])
            P.op("v", lambda v: v.reciprocal(
                S[:, b0 + 6:b0 + 7], S[:, b0 + 5:b0 + 6]),
                 reads=[kp + "sd"], writes=[kp + "r"])

        with contextlib.ExitStack() as sA:
            TA = sA.enter_context
            CT = TA(nc.sbuf_tensor("CT", [128, MC * 128], BF16))
            ST = TA(nc.sbuf_tensor("ST", [128, MC * 128], BF16))
            TRI = TA(nc.sbuf_tensor("TRI", [128, 128], BF16))
            DEN = [TA(nc.sbuf_tensor(f"DEN{i}", [128, 16], F32)) for i in range(2)]
            RINV = [TA(nc.sbuf_tensor(f"RINV{i}", [128, 16], F32)) for i in range(2)]
            OUTH = [TA(nc.sbuf_tensor(f"OUTH{i}", [128, N], BF16)) for i in range(2)]
            AFM = [TA(nc.sbuf_tensor(f"AFM{i}", [128, N], BF16)) for i in range(2)]
            ld(CT[:], AP(ctabh, 0, [[128, 128], [128 * 128, 16], [1, 128]]), "CT")
            ld(ST[:], AP(stabh, 0, [[128, 128], [128 * 128, 16], [1, 128]]), "ST")
            ld(TRI[:], trih.ap(), "TRI")
            XB = [TA(nc.sbuf_tensor(f"XB{i}", [128, E], F32)) for i in range(2)]
            LNC = [TA(nc.sbuf_tensor(f"LNC{i}", [128, E], BF16)) for i in range(2)]
            XS = TA(nc.sbuf_tensor("XS", [128, 8], F32))
            SCR = TA(nc.sbuf_tensor("SCR", [128, E], BF16))
            WH = [TA(nc.sbuf_tensor(f"WH{i}", [128, KC * 384], BF16)) for i in range(2)]
            QR = TA(nc.sbuf_tensor("QR", [128, MC * 256], F32))
            T1 = [TA(nc.sbuf_tensor(f"T1{i}", [128, 256], F32)) for i in range(2)]
            T2 = [TA(nc.sbuf_tensor(f"T2{i}", [128, 256], F32)) for i in range(2)]
            SUMS = TA(nc.sbuf_tensor("SUMS", [128, 32], F32))
            SS = TA(nc.sbuf_tensor("SS", [128, 32], F32))
            MEANS = TA(nc.sbuf_tensor("MEANS", [128, 32], F32))
            RQ = TA(nc.sbuf_tensor("RQ", [128, 32], F32))
            SMA = TA(nc.sbuf_tensor("SMA", [128, 32], F32))
            SMB = TA(nc.sbuf_tensor("SMB", [128, 32], F32))
            QN = TA(nc.sbuf_tensor("QN", [128, MC * 256], BF16))
            P4S = [TA(nc.sbuf_tensor(f"P4S{i}", [128, 128], F32)) for i in range(4)]
            QFM = [TA(nc.sbuf_tensor(f"QFM{i}", [128, N], BF16)) for i in range(2)]
            KFM = [TA(nc.sbuf_tensor(f"KFM{i}", [128, N], BF16)) for i in range(2)]
            VPAD = [TA(nc.sbuf_tensor(f"VPAD{i}", [128, MC * 129], BF16))
                    for i in range(2)]
            S2B = [TA(nc.sbuf_tensor(f"S2B{i}", [128, 512], BF16)) for i in range(4)]
            S4B = [TA(nc.sbuf_tensor(f"S4B{i}", [128, 512], BF16)) for i in range(6)]
            PQ = [TA(nc.psum_tensor(f"PQ{i}", [128, 512], F32)) for i in range(2)]
            PS = [TA(nc.psum_tensor(f"PS{i}", [128, 512], F32)) for i in range(2)]
            PV = [TA(nc.psum_tensor(f"PV{i}", [128, 512], F32)) for i in range(4)]

            for i in range(2):
                P.op("v", lambda v, i=i: v.memset(
                    AP(VPAD[i], 128, [[MC * 129, 128], [129, 16], [1, 1]]), 1.0),
                    writes=[f"vpadones{i}"])


            def emit_w(h):
                par = h % 2
                for t_i, off in enumerate((h * 128, E + h * 128,
                                           2 * E + h * 128)):
                    wk = f"wh{par}/{t_i}"
                    P.dma("g", lambda g, off=off, t_i=t_i, par=par:
                          g.dma_start(
                              out=AP(WH[par], t_i * 128,
                                     [[KC * 384, 128], [384, 16], [1, 128]]),
                              in_=AP(wqkv, off,
                                     [[3 * E, 128], [3 * E * 128, 16],
                                      [1, 128]])),
                          writes=[wk], slot=wk)

            def qkv_units(h):
                par = h % 2
                units = []
                wkeys = [f"wh{par}/{t_i}" for t_i in range(3)]

                def u_m(m):
                    b = m % 2
                    pqk = f"pq{b}"
                    for kc in range(KC):
                        P.op("t", lambda t, m=m, kc=kc, par=par, b=b: t.matmul(
                            PQ[b][:, 0:384],
                            LNXT[:, kc * N + m * 128: kc * N + m * 128 + 128],
                            WH[par][:, kc * 384:(kc + 1) * 384],
                            start=(kc == 0), stop=(kc == KC - 1)),
                            reads=[f"lnxT/{kc}/{m}"] + wkeys, writes=[pqk])
                    P.op("v", lambda v, b=b, m=m: v.tensor_tensor(
                        T1[b][:], PQ[b][:, 0:256],
                        AP(CT, m * 128, [[MC * 128, 128], [0, 2], [1, 128]]),
                        AOT.mult), reads=[pqk, "CT"], writes=[f"t1{b}"])
                    P.op("v", lambda v, b=b, m=m: v.tensor_tensor(
                        T2[b][:],
                        AP(PQ[b], 64, [[512, 128], [128, 2], [-64, 2], [1, 64]]),
                        AP(ST, m * 128, [[MC * 128, 128], [0, 2], [1, 128]]),
                        AOT.mult), reads=[pqk, "ST"], writes=[f"t2{b}"])
                    P.op("v", lambda v, b=b, m=m, par=par: v.tensor_copy(
                        VPAD[par][:, m * 129: m * 129 + 128], PQ[b][:, 256:384]),
                        reads=[pqk], writes=[f"vpad{par}/{m}"])
                    for t_i in range(2):
                        col = 2 * m + t_i
                        P.op("v", lambda v, b=b, m=m, t_i=t_i, col=col:
                             v.scalar_tensor_tensor(
                                 QR[:, m * 256 + t_i * 128:
                                    m * 256 + t_i * 128 + 128],
                                 T1[b][:, t_i * 128:(t_i + 1) * 128], cp(C_ZERO),
                                 T2[b][:, t_i * 128:(t_i + 1) * 128],
                                 AOT.add, AOT.add,
                                 accum_out=SUMS[:, col:col + 1]),
                             reads=[f"t1{b}", f"t2{b}", "CP0"],
                             writes=[f"qr/{m}/{t_i}", f"sums/{col}"])
                        P.op("s", lambda s, m=m, t_i=t_i, col=col: s.activation(
                            P4S[col % 4][:],
                            QR[:, m * 256 + t_i * 128: m * 256 + t_i * 128 + 128],
                            ACTF.Square, accum_out=SS[:, col:col + 1]),
                            reads=[f"qr/{m}/{t_i}"],
                            writes=[f"p4s{col % 4}", f"ssq/{col}"])
                for m in range(MC):
                    units.append(lambda m=m: u_m(m))

                def u_smalls():
                    sumkeys = [f"sums/{c_}" for c_ in range(32)]
                    sskeys = [f"ssq/{c_}" for c_ in range(32)]
                    P.op("v", lambda v: v.tensor_scalar(
                        MEANS[:], SUMS[:], cp(C_I128), None, AOT.mult,
                        AOT.bypass), reads=sumkeys + ["CP1"], writes=["means"])
                    P.op("v", lambda v: v.tensor_scalar(
                        SMA[:], SS[:], cp(C_I128), None, AOT.mult, AOT.bypass),
                        reads=sskeys, writes=["sma"])
                    P.op("v", lambda v: v.tensor_tensor(
                        SMB[:], MEANS[:], MEANS[:], AOT.mult),
                        reads=["means"], writes=["smb"])
                    P.op("v", lambda v: v.tensor_tensor(
                        SMA[:], SMA[:], SMB[:], AOT.subtract),
                        reads=["sma", "smb"], writes=["sma"])
                    P.op("s", lambda s: s.activation(
                        SMB[:], SMA[:], ACTF.Sqrt, bias=cp(C_EPS)),
                        reads=["sma"], writes=["smb"])
                    P.op("v", lambda v: v.reciprocal(RQ[:], SMB[:]),
                         reads=["smb"], writes=["rq"])
                units.append(u_smalls)

                def u_norm(m):
                    for t_i in range(2):
                        col = 2 * m + t_i
                        P.op("v", lambda v, m=m, t_i=t_i, col=col:
                             v.tensor_scalar(
                                 QN[:, t_i * N + m * 128:
                                    t_i * N + m * 128 + 128],
                                 QR[:, m * 256 + t_i * 128:
                                    m * 256 + t_i * 128 + 128],
                                 MEANS[:, col:col + 1], RQ[:, col:col + 1],
                                 AOT.subtract, AOT.mult),
                             reads=[f"qr/{m}/{t_i}", "means", "rq"],
                             writes=[f"qn/{m}/{t_i}"])
                for m in range(MC):
                    units.append(lambda m=m: u_norm(m))

                def u_final():
                    qnkeys = [f"qn/{m}/{t}" for m in range(MC)
                              for t in range(2)]
                    for t_i, dst, nm in ((0, QFM, "qfm"), (1, KFM, "kfm")):
                        P.dma("y", lambda y, t_i=t_i, dst=dst, par=par:
                              y.dma_start_transpose(
                                  AP(dst[par], 0,
                                     [[N, 128], [128, 16], [1, 128]]),
                                  QN[:, t_i * N:(t_i + 1) * N]),
                              reads=qnkeys,
                              writes=[f"{nm}{par}/{m}" for m in range(MC)],
                              slot=f"qnT{t_i}")
                    qk = [f"qfm{par}/{m}" for m in range(MC)]
                    kk = [f"kfm{par}/{m}" for m in range(MC)]
                    P.op("s", lambda s, par=par: s.activation(
                        QFM[par][:], QFM[par][:], ACTF.Identity, bias=BET[:],
                        scale=GAM[:]),
                        reads=qk + ["GAM", "BET"], writes=[f"qfmF{par}"] + qk)
                    P.op("s", lambda s, par=par: s.activation(
                        KFM[par][:], KFM[par][:], ACTF.Identity, bias=BET[:],
                        scale=GAM[:]),
                        reads=kk + ["GAM", "BET"], writes=[f"kfmF{par}"] + kk)
                units.append(u_final)
                return units

            def attn_units(hh):
                par = hh % 2
                vkeys = [f"vpad{par}/{m}" for m in range(MC)] + [f"vpadones{par}"]
                units = []

                def u_s(i, j):
                    r = j - 4 * i
                    s0 = 128 * max(0, r)
                    nj = 512 - s0
                    jb = j % 6
                    pb_ = j % 2
                    psk = f"ps{pb_}"
                    P.op("t", lambda t, i=i, j=j, s0=s0, nj=nj, pb_=pb_,
                         par=par: t.matmul(
                             PS[pb_][:, 0:nj],
                             KFM[par][:, j * 128:(j + 1) * 128],
                             QFM[par][:, i * 512 + s0:(i + 1) * 512],
                             start=True, stop=True),
                         reads=[f"qfmF{par}", f"kfmF{par}"], writes=[psk])
                    j2 = j % 4
                    P.op("s", lambda s, j2=j2, pb_=pb_, nj=nj: s.activation(
                        S2B[j2][:, 0:nj], PS[pb_][:, 0:nj], ACTF.Square),
                        reads=[psk], writes=[f"s2{j2}"])
                    P.op("s", lambda s, jb=jb, j2=j2, nj=nj: s.activation(
                        S4B[jb][:, 0:nj], S2B[j2][:, 0:nj], ACTF.Square),
                        reads=[f"s2{j2}"], writes=[f"s4{jb}"])
                    if r >= 0:
                        P.op("v", lambda v, jb=jb: v.tensor_tensor(
                            S4B[jb][:, 0:128], S4B[jb][:, 0:128], TRI[:],
                            AOT.mult),
                            reads=[f"s4{jb}", "TRI"], writes=[f"s4{jb}"])

                def u_pv(i, j):
                    r = j - 4 * i
                    s0 = 128 * max(0, r)
                    jb = j % 6
                    for sub in range(max(0, r), 4):
                        P.op("t", lambda t, jb=jb, sub=sub, s0=s0, j=j,
                             par=par, i=i: t.matmul(
                                 PV[sub][:, 0:129],
                                 S4B[jb][:, 128 * sub - s0:
                                         128 * sub - s0 + 128],
                                 VPAD[par][:, j * 129: j * 129 + 129],
                                 start=(j == 0), stop=(j == 4 * i + sub)),
                             reads=[f"s4{jb}"] + vkeys, writes=[f"pv{sub}"])

                def u_epilogue(i):
                    for sub in range(4):
                        P.op("v", lambda v, sub=sub, i=i, par=par: v.tensor_copy(
                            DEN[par][:, i * 4 + sub: i * 4 + sub + 1],
                            PV[sub][:, 128:129]),
                            reads=[f"pv{sub}"], writes=[f"den{par}/{i}/{sub}"])
                    dk = [f"den{par}/{i}/{s_}" for s_ in range(4)]
                    P.op("s", lambda s, i=i, par=par: s.activation(
                        RINV[par][:, i * 4: i * 4 + 4],
                        DEN[par][:, i * 4: i * 4 + 4], ACTF.Ln),
                        reads=dk, writes=[f"lnden{par}/{i}"])
                    P.op("s", lambda s, i=i, par=par: s.activation(
                        RINV[par][:, i * 4: i * 4 + 4],
                        RINV[par][:, i * 4: i * 4 + 4], ACTF.Exp, scale=-1.0),
                        reads=[f"lnden{par}/{i}"], writes=[f"rinvA{par}/{i}"])
                    for sub in range(4):
                        blk = i * 4 + sub
                        P.op("v", lambda v, sub=sub, blk=blk, par=par, i=i:
                             v.tensor_scalar(
                                 OUTH[par][:, blk * 128:(blk + 1) * 128],
                                 PV[sub][:, 0:128],
                                 RINV[par][:, blk:blk + 1], None,
                                 AOT.mult, AOT.bypass),
                             reads=[f"pv{sub}", f"rinvA{par}/{i}"],
                             writes=[f"outh{par}/{blk}"])

                def u_spill():
                    P.dma("y", lambda y, par=par: y.dma_start_transpose(
                        AP(AFM[par], 0, [[N, 128], [128, 16], [1, 128]]),
                        AP(OUTH[par], 0, [[N, 128], [128, 16], [1, 128]])),
                        reads=[f"outh{par}/{b_}" for b_ in range(16)],
                        writes=[f"afm{par}/{b_}" for b_ in range(16)],
                        slot=f"afmT{par}")
                    akeys = [f"afm{par}/{b_}" for b_ in range(16)]
                    P.dma("g", lambda g, hh=hh, par=par: g.dma_start(
                        out=attnt_h[hh * 128:(hh + 1) * 128, :],
                        in_=AFM[par][:]),
                        reads=akeys, writes=[f"attnsp/{hh}"],
                        slot=f"attnsp{par}")

                LAG = 4
                for i in range(NB):
                    nj_ = 4 * i + 4
                    for j in range(nj_):
                        def u_sj(i=i, j=j):
                            u_s(i, j)
                            if j >= LAG:
                                u_pv(i, j - LAG)
                        units.append(u_sj)

                    def u_end(i=i, nj_=nj_):
                        for j in range(max(0, nj_ - LAG), nj_):
                            u_pv(i, j)
                        u_epilogue(i)
                    units.append(u_end)
                units.append(u_spill)
                return units

            XB0B = XB[0].bitcast(BF16)
            XB1B = XB[1].bitcast(BF16)
            P.op("v", lambda v: v.memset(ALPHACC[:], 0.0), writes=["alacc"])

            def wc_plane(kc):
                return XB0B[:, 0:E] if kc % 2 == 0 else XB1B[:, E:2 * E]

            def swt_load(kc):
                wp = kc % 2
                P.dma("g", lambda g, kc=kc: g.dma_start(
                    out=wc_plane(kc),
                    in_=AP(fw, kc * 128, [[E, 128], [E * 128, 16], [1, 128]])),
                    writes=[f"wcA{wp}", "xb0" if wp == 0 else "xb1"],
                    slot=f"wcA{wp}")

            def swt_unit(kc):
                wp = kc % 2
                wpl = wc_plane(kc)
                P.op("v", lambda v, wpl=wpl: v.tensor_scalar(
                    XB1B[:, 0:E], wpl, cp(C_ZERO), None, AOT.is_ge,
                    AOT.bypass),
                    reads=[f"wcA{wp}", "CP0"], writes=["sgA", "xb1"])
                P.op("v", lambda v: v.tensor_scalar(
                    XB1B[:, 0:E], XB1B[:, 0:E], cp(C_TWO), cp(C_ONE), AOT.mult,
                    AOT.subtract),
                    reads=["sgA", "CP4", "CP5"], writes=["sgA"])
                P.op("v", lambda v, wpl=wpl: v.tensor_reduce(
                    ALPHA[:], AP(wpl.tensor, wpl.offset,
                                 [[2 * E, 128], [128, 16], [1, 128]]),
                    AX.X, AOT.add, apply_absolute_value=True),
                    reads=[f"wcA{wp}"], writes=["alpart"])
                P.op("v", lambda v: v.tensor_tensor(
                    ALPHACC[:], ALPHACC[:], ALPHA[:], AOT.add),
                    reads=["alpart", "alacc"], writes=["alacc"])
                P.dma("y", lambda y: y.dma_start_transpose(
                    AP(XB0B, E, [[2 * E, 128], [128, 16], [1, 128]]),
                    XB1B[:, 0:E]),
                    reads=["sgA"], writes=["stgA", "xb0"], slot="swtT")
                P.dma("g", lambda g, kc=kc: g.dma_start(
                    out=swt_h[kc * 128:(kc + 1) * 128, :], in_=XB0B[:, E:2 * E]),
                    reads=["stgA"], writes=[f"swtsp/{kc}"], slot="swtsp")

            lnxt_keys = [f"lnxT/{kc}/{c}" for kc in range(KC)
                         for c in range(MC)]
            emit_w(0)
            emit_w(1)

            uq0 = qkv_units(0)
            for c in range(MC):
                b = c % 2
                xk = f"xb{b}"
                P.dma("g", lambda g, c=c, b=b: g.dma_start(
                    out=XB[b][:], in_=xh[c * 128:(c + 1) * 128, :]),
                    writes=[xk], slot=xk)
                P.op("v", lambda v, b=b: v.tensor_reduce(
                    XS[:, 0:1], XB[b][:], AX.X, AOT.add),
                    reads=[xk], writes=["xsum"])
                P.op("s", lambda s, b=b: s.activation(
                    SCR[:], XB[b][:], ACTF.Square, accum_out=XS[:, 1:2]),
                    reads=[xk], writes=["xss", "scrA"])
                ln_smalls(XS, "x", "xsum", "xss")
                P.op("v", lambda v: v.tensor_scalar(
                    XS[:, 7:8], XS[:, 0:1], cp(C_NI2048), None,
                    AOT.mult, AOT.bypass),
                    reads=["xsum", "CP6"], writes=["xnm"])
                P.op("v", lambda v: v.tensor_tensor(
                    XS[:, 7:8], XS[:, 7:8], XS[:, 6:7], AOT.mult),
                    reads=["xnm", "xr"], writes=["xnm"])
                P.op("s", lambda s, b=b: s.activation(
                    LNC[b][:], XB[b][:], ACTF.Identity, bias=XS[:, 7:8],
                    scale=XS[:, 6:7]),
                    reads=[xk, "xr", "xnm"], writes=[f"lnc{b}"])
                P.dma("y", lambda y, c=c, b=b: y.dma_start_transpose(
                    AP(LNXT, c * 128, [[KC * N, 128], [N, 16], [1, 128]]),
                    AP(LNC[b], 0, [[E, 128], [128, 16], [1, 128]])),
                    reads=[f"lnc{b}"],
                    writes=[f"lnxT/{kc}/{c}" for kc in range(KC)],
                    slot=f"lnxTd{c % 4}")
                if c >= 2:
                    uq0[c - 2]()
            for th in uq0[MC - 2:]:
                th()
            emit_w(2)
            swt_load(0)

            for h in range(1, H + 1):
                uq = qkv_units(h) if h < H else []
                ua = attn_units(h - 1) if h >= 1 else []
                ia = 0
                SHIFT = 3
                for iq, th in enumerate(uq):
                    th()
                    want = int(round((iq + 1 - SHIFT) * len(ua)
                                     / max(1, len(uq) - SHIFT)))
                    while ia < min(max(0, want), len(ua)):
                        ua[ia]()
                        ia += 1
                while ia < len(ua):
                    ua[ia]()
                    ia += 1
                if h + 2 < H:
                    emit_w(h + 2)
                if h == 0:
                    swt_load(0)
                elif h < H:
                    swt_unit(h - 1)
                    swt_load(h)
                else:
                    swt_unit(H - 1)
                if h == H - 1:
                    P.dma("g", lambda g: g.dma_start(
                        out=AP(LNXT, 0, [[KC * N, 128], [N, 16], [1, N]]),
                        in_=AP(wout, 0, [[E, 128], [E * 128, 16], [1, E]])),
                        writes=lnxt_keys + ["woutP"], slot="woutP")
            P.op("v", lambda v: v.tensor_scalar(
                ALPHA[:], ALPHACC[:], cp(C_I2048), None, AOT.mult, AOT.bypass),
                reads=["alacc", "CP2"], writes=["alphaF", "alpart"])

        P.fence()

        with contextlib.ExitStack() as sB:
            TB = sB.enter_context
            ATK = [TB(nc.sbuf_tensor(f"ATK{i}", [128, KC * 512], BF16))
                   for i in range(2)]
            XB2 = [TB(nc.sbuf_tensor(f"XB2{i}", [128, E], F32)) for i in range(4)]
            APJT = [TB(nc.sbuf_tensor(f"APJT{i}", [128, 512], BF16))
                    for i in range(4)]
            APJ = [TB(nc.sbuf_tensor(f"APJ{i}", [128, 4 * E], BF16))
                   for i in range(2)]
            SCB = TB(nc.sbuf_tensor("SCB", [128, E], BF16))
            O1 = [TB(nc.sbuf_tensor(f"O1{i}", [128, E], F32)) for i in range(2)]
            LN2C = [TB(nc.sbuf_tensor(f"LN2C{i}", [128, E], BF16))
                    for i in range(2)]
            L2F = [TB(nc.sbuf_tensor(f"L2F{i}", [128, KC * 128], BF16))
                   for i in range(2)]
            BS = TB(nc.sbuf_tensor("BS", [128, 32], F32))
            PB = [TB(nc.psum_tensor(f"PB{i}", [128, 512], F32)) for i in range(4)]

            P.barrier("g", ["g:attnsp0", "g:attnsp1"], writes=["attn_done"])

            def load_atk(tb):
                tbb = tb % 2
                P.dma("g", lambda g, tb=tb, tbb=tbb: g.dma_start(
                    out=ATK[tbb][:],
                    in_=AP(attnt_h, tb * 512,
                           [[N, 128], [N * 128, 16], [1, 512]])),
                    reads=["attn_done"], writes=[f"atk{tbb}"], slot=f"atk{tbb}")

            def b_s_unit(tb, s_):
                tbb = tb % 2
                pb = s_ % 4
                for kc in range(KC):
                    P.op("t", lambda t, s_=s_, kc=kc, pb=pb, tbb=tbb: t.matmul(
                        PB[pb][:],
                        LNXT[:, kc * N + s_ * 128: kc * N + s_ * 128 + 128],
                        ATK[tbb][:, kc * 512:(kc + 1) * 512],
                        start=(kc == 0), stop=(kc == KC - 1)),
                        reads=["woutP", f"atk{tbb}"], writes=[f"pb{pb}"])
                P.op("s", lambda s, pb=pb: s.activation(
                    APJT[pb][:], PB[pb][:], ACTF.Identity),
                    reads=[f"pb{pb}"], writes=[f"apjT{pb}"])
                P.dma("y", lambda y, s_=s_, pb=pb, tbb=tbb:
                      y.dma_start_transpose(
                          AP(APJ[tbb], s_ * 128,
                             [[4 * E, 128], [E, 4], [1, 128]]),
                          AP(APJT[pb], 0, [[512, 128], [128, 4], [1, 128]])),
                      reads=[f"apjT{pb}"],
                      writes=[f"apj{tbb}/{cl}/{s_}" for cl in range(4)],
                      slot=f"apjTd{pb}")

            def b_cl_unit(tb, cl):
                tbb = tb % 2
                c = tb * 4 + cl
                cb = c % 2
                b0 = cl * 8
                kp = f"B{cl}"
                apk = [f"apj{tbb}/{cl}/{s_}" for s_ in range(16)]
                P.op("v", lambda v, cl=cl, tbb=tbb, b0=b0: v.tensor_reduce(
                    BS[:, b0:b0 + 1], APJ[tbb][:, cl * E:(cl + 1) * E], AX.X,
                    AOT.add), reads=apk, writes=[kp + "sum"])
                P.op("s", lambda s, cl=cl, tbb=tbb, b0=b0: s.activation(
                    SCB[:], APJ[tbb][:, cl * E:(cl + 1) * E], ACTF.Square,
                    accum_out=BS[:, b0 + 1:b0 + 2]),
                    reads=apk, writes=[kp + "ss", "scb"])
                ln_smalls(BS, kp, kp + "sum", kp + "ss", b0=b0)
                P.op("v", lambda v, cl=cl, tbb=tbb, cb=cb, b0=b0:
                     v.tensor_scalar(
                         O1[cb][:], APJ[tbb][:, cl * E:(cl + 1) * E],
                         BS[:, b0 + 2:b0 + 3], BS[:, b0 + 6:b0 + 7],
                         AOT.subtract, AOT.mult),
                     reads=apk + [kp + "m", kp + "r"], writes=[f"o1{cb}"])
                P.op("v", lambda v, cb=cb, c=c, b0=b0: v.scalar_tensor_tensor(
                    O1[cb][:], O1[cb][:], cp(C_ZERO), XB2[c % 4][:], AOT.add,
                    AOT.add, accum_out=BS[:, b0:b0 + 1]),
                    reads=[f"o1{cb}", f"xb2{c % 4}", "CP0"],
                    writes=[f"o1{cb}", kp + "sum"])
                P.dma("g", lambda g, c=c, cb=cb: g.dma_start(
                    out=out1_h[c * 128:(c + 1) * 128, :], in_=O1[cb][:]),
                    reads=[f"o1{cb}"], writes=[f"o1sp/{c}"], slot=f"o1sp{cb}")
                P.op("s", lambda s, cb=cb, b0=b0: s.activation(
                    SCB[:], O1[cb][:], ACTF.Square,
                    accum_out=BS[:, b0 + 1:b0 + 2]),
                    reads=[f"o1{cb}"], writes=[kp + "ss", "scb"])
                ln_smalls(BS, kp + "L2", kp + "sum", kp + "ss", b0=b0)
                P.op("v", lambda v, cb=cb, b0=b0: v.tensor_scalar(
                    LN2C[cb][:], O1[cb][:], BS[:, b0 + 2:b0 + 3],
                    BS[:, b0 + 6:b0 + 7], AOT.subtract, AOT.mult),
                    reads=[f"o1{cb}", kp + "L2m", kp + "L2r"],
                    writes=[f"ln2c{cb}"])
                P.dma("y", lambda y, cb=cb: y.dma_start_transpose(
                    AP(L2F[cb], 0, [[KC * 128, 128], [128, 16], [1, 128]]),
                    AP(LN2C[cb], 0, [[E, 128], [128, 16], [1, 128]])),
                    reads=[f"ln2c{cb}"],
                    writes=[f"l2f{cb}/{kc}" for kc in range(KC)],
                    slot=f"l2fT{cb}")
                P.dma("g", lambda g, c=c, cb=cb: g.dma_start(
                    out=AP(lnx2t_h, c * 128,
                           [[N, 128], [N * 128, 16], [1, 128]]),
                    in_=L2F[cb][:]),
                    reads=[f"l2f{cb}/{kc}" for kc in range(KC)],
                    writes=[f"l2sp/{c}"], slot=f"l2sp{cb}")

            def load_xb2(c):
                P.dma("g", lambda g, c=c: g.dma_start(
                    out=XB2[c % 4][:], in_=xh[c * 128:(c + 1) * 128, :]),
                    writes=[f"xb2{c % 4}"], slot=f"xb2{c % 4}")

            load_atk(0)
            for cl in range(4):
                load_xb2(cl)
            for tb in range(4):
                if tb + 1 < 4:
                    load_atk(tb + 1)
                for s_ in range(16):
                    b_s_unit(tb, s_)
                    if tb >= 1 and s_ % 4 == 3:
                        cl = s_ // 4
                        b_cl_unit(tb - 1, cl)
                        load_xb2(tb * 4 + cl)
            for cl in range(4):
                b_cl_unit(3, cl)

            P.dma("g", lambda g: g.dma_start(
                out=AP(LNXT, 0, [[KC * N, 128], [N, 16], [1, N]]),
                in_=AP(swt_h, 0, [[N, 128], [N * 128, 16], [1, N]])),
                reads=[f"swtsp/{kc}" for kc in range(KC)],
                writes=["woutP", "swtP"], slot="swtP")

        P.fence()

        with contextlib.ExitStack() as sC:
            TC = sC.enter_context
            LX = [TC(nc.sbuf_tensor(f"LX{i}", [128, KC * 512], BF16))
                  for i in range(2)]
            YTT = [TC(nc.sbuf_tensor(f"YTT{i}", [128, 512], BF16))
                   for i in range(4)]
            YTM = [TC(nc.sbuf_tensor(f"YTM{i}", [128, 4 * E], BF16))
                   for i in range(2)]
            O1R = [TC(nc.sbuf_tensor(f"O1R{i}", [128, E], F32)) for i in range(4)]
            OC = [TC(nc.sbuf_tensor(f"OC{i}", [128, E], F32)) for i in range(2)]
            SCC = TC(nc.sbuf_tensor("SCC", [128, E], BF16))
            CS = TC(nc.sbuf_tensor("CS", [128, 32], F32))
            PC = [TC(nc.psum_tensor(f"PC{i}", [128, 512], F32)) for i in range(4)]

            P.barrier("g", ["g:l2sp0", "g:l2sp1"], writes=["l2_done"])
            P.barrier("g", ["g:o1sp0", "g:o1sp1"], writes=["o1_done"])

            def load_lx(tb):
                tbb = tb % 2
                P.dma("g", lambda g, tb=tb, tbb=tbb: g.dma_start(
                    out=LX[tbb][:],
                    in_=AP(lnx2t_h, tb * 512,
                           [[N, 128], [N * 128, 16], [1, 512]])),
                    reads=["l2_done"], writes=[f"lx{tbb}"], slot=f"lx{tbb}")

            def c_s_unit(tb, s_):
                tbb = tb % 2
                pb = s_ % 4
                for kc in range(KC):
                    P.op("t", lambda t, s_=s_, kc=kc, pb=pb, tbb=tbb: t.matmul(
                        PC[pb][:],
                        LNXT[:, kc * N + s_ * 128: kc * N + s_ * 128 + 128],
                        LX[tbb][:, kc * 512:(kc + 1) * 512],
                        start=(kc == 0), stop=(kc == KC - 1)),
                        reads=["swtP", f"lx{tbb}"], writes=[f"pc{pb}"])
                P.op("s", lambda s, s_=s_, pb=pb: s.activation(
                    YTT[pb][:], PC[pb][:], ACTF.Identity,
                    bias=BIASF[:, s_:s_ + 1], scale=ALPHA[:, s_:s_ + 1]),
                    reads=[f"pc{pb}", "alphaF", "BIASF"], writes=[f"ytt{pb}"])
                P.dma("y", lambda y, s_=s_, pb=pb, tbb=tbb:
                      y.dma_start_transpose(
                          AP(YTM[tbb], s_ * 128,
                             [[4 * E, 128], [E, 4], [1, 128]]),
                          AP(YTT[pb], 0, [[512, 128], [128, 4], [1, 128]])),
                      reads=[f"ytt{pb}"],
                      writes=[f"ytm{tbb}/{cl}/{s_}" for cl in range(4)],
                      slot=f"ytmTd{pb}")

            def c_cl_unit(tb, cl):
                tbb = tb % 2
                c = tb * 4 + cl
                cb = c % 2
                b0 = cl * 8
                kp = f"C{cl}"
                ymk = [f"ytm{tbb}/{cl}/{s_}" for s_ in range(16)]
                P.op("v", lambda v, cl=cl, tbb=tbb, b0=b0: v.tensor_reduce(
                    CS[:, b0:b0 + 1], YTM[tbb][:, cl * E:(cl + 1) * E], AX.X,
                    AOT.add), reads=ymk, writes=[kp + "sum"])
                P.op("s", lambda s, cl=cl, tbb=tbb, b0=b0: s.activation(
                    SCC[:], YTM[tbb][:, cl * E:(cl + 1) * E], ACTF.Square,
                    accum_out=CS[:, b0 + 1:b0 + 2]),
                    reads=ymk, writes=[kp + "ss", "sccS"])
                ln_smalls(CS, kp, kp + "sum", kp + "ss", b0=b0)
                P.op("v", lambda v, cl=cl, tbb=tbb, cb=cb, b0=b0:
                     v.tensor_scalar(
                         OC[cb][:], YTM[tbb][:, cl * E:(cl + 1) * E],
                         CS[:, b0 + 2:b0 + 3], CS[:, b0 + 6:b0 + 7],
                         AOT.subtract, AOT.mult),
                     reads=ymk + [kp + "m", kp + "r"], writes=[f"oc{cb}"])
                P.op("v", lambda v, cb=cb, c=c: v.tensor_tensor(
                    OC[cb][:], OC[cb][:], O1R[c % 4][:], AOT.add),
                    reads=[f"oc{cb}", f"o1r{c % 4}"], writes=[f"oc{cb}"])
                P.dma("g", lambda g, c=c, cb=cb: g.dma_start(
                    out=out_h[c * 128:(c + 1) * 128, :], in_=OC[cb][:]),
                    reads=[f"oc{cb}"], writes=[f"outsp/{c}"],
                    slot=f"outsp{cb}")

            def load_o1r(c):
                P.dma("g", lambda g, c=c: g.dma_start(
                    out=O1R[c % 4][:], in_=out1_h[c * 128:(c + 1) * 128, :]),
                    reads=["o1_done"], writes=[f"o1r{c % 4}"],
                    slot=f"o1r{c % 4}")

            load_lx(0)
            for cl in range(4):
                load_o1r(cl)
            for tb in range(4):
                if tb + 1 < 4:
                    load_lx(tb + 1)
                for s_ in range(16):
                    c_s_unit(tb, s_)
                    if tb >= 1 and s_ % 4 == 3:
                        cl = s_ // 4
                        c_cl_unit(tb - 1, cl)
                        load_o1r(tb * 4 + cl)
            for cl in range(4):
                c_cl_unit(3, cl)

            P.barrier("g", ["g:outsp0", "g:outsp1"], writes=["all_done"])

            P.finalize_marks()
            with contextlib.ExitStack() as se:
                SE = se.enter_context
                sems = {e: SE(nc.semaphore(f"sem_{e}")) for e in ("t", "v", "s")}
                slot_sems = {}
                for i_, sk in enumerate(P.slots):
                    slot_sems[sk] = SE(nc.semaphore(f"sd{i_}"))
                block = SE(nc.Block())

                @block.tensor
                def _(t):
                    P.emit_one("t", t, sems, slot_sems)

                @block.vector
                def _(v):
                    P.emit_one("v", v, sems, slot_sems)

                @block.scalar
                def _(s):
                    P.emit_one("s", s, sems, slot_sems)

                @block.gpsimd
                def _(g):
                    P.emit_one("g", g, sems, slot_sems)

                @block.sync
                def _(y):
                    P.emit_one("y", y, sems, slot_sems)

    return nc


_CACHE = {}


def kernel(L0_attn_x, L0_attn_w_qkv, L0_attn_w_out, L0_attn_ln_scale,
           L0_attn_ln_bias, L0_ffn_weight, L0_ffn_bias):
    x = np.ascontiguousarray(np.asarray(L0_attn_x, dtype=np.float32))
    consts = host_consts()
    if "nc" not in _CACHE:
        _CACHE["nc"] = build_nc()
    nc = _CACHE["nc"]
    base = {
        "wqkv": np.ascontiguousarray(np.asarray(L0_attn_w_qkv, np.float32)),
        "wout": np.ascontiguousarray(np.asarray(L0_attn_w_out, np.float32)),
        "gam": np.ascontiguousarray(np.asarray(L0_attn_ln_scale, np.float32)),
        "bet": np.ascontiguousarray(np.asarray(L0_attn_ln_bias, np.float32)),
        "fw": np.ascontiguousarray(np.asarray(L0_ffn_weight, np.float32)),
        "fb": np.ascontiguousarray(np.asarray(L0_ffn_bias, np.float32)),
        "ctab": consts["ctab"], "stab": consts["stab"], "tri": consts["tri"],
    }
    in_maps = [dict(base, xb=np.ascontiguousarray(x[b])) for b in range(8)]
    res = run_bass_kernel_spmd(nc, in_maps, core_ids=list(range(8)))
    return np.stack([np.asarray(res.results[b]["out"], np.float32)
                     for b in range(8)], axis=0)
